# version 05 (final)
# speedup vs baseline: 1.2252x; 1.0146x over previous
import contextlib
import numpy as np
import ml_dtypes
import concourse.bass as bass
import concourse.mybir as mybir
from concourse.bass_utils import run_bass_kernel_spmd





F32 = mybir.dt.float32
BF16 = mybir.dt.bfloat16
I32 = mybir.dt.int32
AF = mybir.ActivationFunctionType
ALU = mybir.AluOpType
AX = mybir.AxisListType.X

NCORE = 8
S = 2048
D = 1024
NT = S // 128
EPS_LN = 1e-5
DI = 2816


def chunks_of(total, step, base=0):
    out, o = [], 0
    while o < total:
        w = min(step, total - o)
        out.append((base + o, w))
        o += w
    return out


class Prog:
    def __init__(self):
        self.nc = bass.Bass("TRN2", target_bir_lowering=False)
        self.ctx = contextlib.ExitStack()
        self._regions = []
        self._n = 0

    def push_region(self):
        st = contextlib.ExitStack()
        self._regions.append(st)
        return st

    def pop_region(self):
        self._regions.pop().close()

    def uname(self, p):
        self._n += 1
        return f"{p}{self._n}"

    def sb(self, shape, dt=F32, name=None):
        st = self._regions[-1] if self._regions else self.ctx
        return st.enter_context(
            self.nc.sbuf_tensor(name or self.uname("sb"), list(shape), dt))

    def ps(self, shape, dt=F32, name=None):
        st = self._regions[-1] if self._regions else self.ctx
        return st.enter_context(
            self.nc.psum_tensor(name or self.uname("ps"), list(shape), dt))

    def sem(self, name=None):
        return self.ctx.enter_context(self.nc.semaphore(name or self.uname("sm")))

    def din(self, name, shape, dt=F32):
        return self.nc.dram_tensor(name, list(shape), dt, kind="ExternalInput").ap()

    def dout(self, name, shape, dt=F32):
        return self.nc.dram_tensor(name, list(shape), dt, kind="ExternalOutput").ap()

    def dscr(self, name, shape, dt=F32):
        return self.nc.dram_tensor(name, list(shape), dt).ap()

    def constf(self, val):
        if not hasattr(self, "_consts"):
            self._consts = {}
        if val in self._consts:
            return self._consts[val]
        t = self.sb([128, 1], F32)
        with self.nc.Block() as blk:
            @blk.gpsimd
            def _(g):
                g = self.chain(g)
                g.memset(t[:, :], float(val))
        self._consts[val] = t
        return t

    def chain(self, eng):
        return _ChainProxy(self, eng)

    def dsem(self, n=2, group=1, name=None):
        return DSem(self, n, group, name)

    def close(self):
        self.ctx.close()
        return self.nc



class DSem:

    def __init__(self, P, n=2, group=1, name=None):
        self.sems = [P.sem((name or P.uname("ds")) + f"_{i}") for i in range(n)]
        self.n = n
        self.group = group

    def sem(self, k):
        return self.sems[k % self.n]

    def wait(self, eng, k):
        eng.wait_ge(self.sems[k % self.n], 16 * self.group * (k // self.n + 1))

    def wait_all(self, eng, n_issues):
        for i in range(min(self.n, n_issues)):
            cnt = (n_issues - i + self.n - 1) // self.n
            eng.wait_ge(self.sems[i], 16 * self.group * cnt)


_CHAIN_OPS = {
    "memset", "memzero", "tensor_copy", "tensor_tensor", "tensor_add",
    "tensor_sub", "tensor_mul", "tensor_max", "tensor_scalar",
    "tensor_single_scalar", "tensor_scalar_add", "tensor_scalar_sub",
    "tensor_scalar_mul", "tensor_scalar_max", "tensor_scalar_min",
    "tensor_relu", "tensor_reduce", "reduce_max", "reduce_sum",
    "scalar_tensor_tensor", "tensor_tensor_scan", "reciprocal", "select",
    "copy_predicated", "transpose", "iota", "activation", "add", "mul",
    "copy", "sign", "sqrt", "square", "bn_stats", "bn_aggr", "pool",
    "pool_avg", "pool_max", "stream_shuffle",
}


class _InstShim:
    __slots__ = ("_inst", "_st")

    def __init__(self, inst, st):
        self._inst = inst
        self._st = st

    def then_inc(self, sem, n=1):
        self._inst.then_inc(sem, n)
        st = self._st
        key = sem.name
        st["counts"][key] = st["counts"].get(key, 0) + n
        st["last"] = (sem, st["counts"][key])
        if st["pending"] is self._inst:
            st["pending"] = None
        return self

    def __getattr__(self, a):
        return getattr(self._inst, a)


class _ChainProxy:

    def __init__(self, P, eng):
        self._P = P
        self._eng = eng
        name = str(getattr(eng, "engine", id(eng)))
        states = P.__dict__.setdefault("_chain_states", {})
        if name not in states:
            states[name] = {
                "self_sem": P.sem(),
                "self_cnt": 0,
                "counts": P.__dict__.setdefault("_chain_counts", {}),
                "last": None,
                "pending": None,
            }
        self._st = states[name]

    def _emit(self, opname, *args, **kwargs):
        st = self._st
        if st["pending"] is not None:
            st["pending"].then_inc(st["self_sem"], 1)
            st["self_cnt"] += 1
            st["last"] = (st["self_sem"], st["self_cnt"])
            st["pending"] = None
        if st["last"] is not None:
            self._eng.wait_ge(st["last"][0], st["last"][1])
        inst = getattr(self._eng, opname)(*args, **kwargs)
        st["pending"] = inst
        return _InstShim(inst, st)

    def __getattr__(self, a):
        if a in _CHAIN_OPS:
            return lambda *args, **kw: self._emit(a, *args, **kw)
        return getattr(self._eng, a)


def emit_ident(P):
    nc = P.nc
    P.ident = P.sb([128, 128], BF16, "ident")
    P.ones_row = P.sb([1, 128], BF16, "ones_row")
    colf = P.sb([128, 128], F32)
    rowf = P.sb([128, 1], F32)
    with nc.Block() as blk:
        @blk.gpsimd
        def _(g):
            g = P.chain(g)
            g.iota(colf[:, :], [[1, 128]], channel_multiplier=0,
                   allow_small_or_imprecise_dtypes=True)
            g.iota(rowf[:, :], [[1, 1]], channel_multiplier=1,
                   allow_small_or_imprecise_dtypes=True)
            g.memset(P.ones_row[:, :], 1.0)
    with nc.Block() as blk:
        @blk.vector
        def _(v):
            v = P.chain(v)
            v.tensor_single_scalar(P.ident[:, :], colf[:, :], rowf[:, 0:1],
                                   ALU.is_equal)


def emit_ln_stats(P, x_sb, n_tiles, width, mean_sb, rstd_sb, x_col0=0):
    nc = P.nc
    eps_t = P.constf(EPS_LN)
    sq = P.sb([128, 2 * width], F32)
    tmp = P.sb([128, 8], F32)
    var = P.sb([128, n_tiles], F32)
    s_sq = P.sem()
    s_rd = P.sem()
    s_var = P.sem()
    with nc.Block() as blk:
        @blk.scalar
        def _(s):
            s = P.chain(s)
            for tt in range(n_tiles):
                if tt >= 2:
                    s.wait_ge(s_rd, tt - 1)
                s.activation(sq[:, (tt % 2) * width:(tt % 2 + 1) * width],
                             x_sb[:, x_col0 + tt * width: x_col0 + (tt + 1) * width],
                             AF.Square).then_inc(s_sq, 1)
            for tt in range(n_tiles):
                s.wait_ge(s_var, tt + 1)
                s.activation(var[:, tt:tt + 1], var[:, tt:tt + 1], AF.Ln,
                             bias=eps_t[:, 0:1])
            for tt in range(n_tiles):
                s.activation(rstd_sb[:, tt:tt + 1], var[:, tt:tt + 1], AF.Exp,
                             scale=-0.5)

        @blk.vector
        def _(v):
            v = P.chain(v)
            def emit_iter(tt, prev):
                if tt is not None:
                    p = (tt % 2) * 4
                    v.wait_ge(s_sq, tt + 1)
                    xa = x_sb[:, x_col0 + tt * width: x_col0 + (tt + 1) * width]
                    v.reduce_sum(tmp[:, p:p + 1], xa, axis=AX)
                    v.reduce_sum(tmp[:, p + 1:p + 2],
                                 sq[:, (tt % 2) * width:(tt % 2 + 1) * width],
                                 axis=AX).then_inc(s_rd, 1)
                if prev is not None:
                    q = (prev % 2) * 4
                    v.tensor_mul(tmp[:, q + 2:q + 3], mean_sb[:, prev:prev + 1],
                                 mean_sb[:, prev:prev + 1])
                if tt is not None:
                    p = (tt % 2) * 4
                    v.tensor_scalar_mul(mean_sb[:, tt:tt + 1], tmp[:, p:p + 1],
                                        1.0 / width)
                if prev is not None:
                    q = (prev % 2) * 4
                    v.scalar_tensor_tensor(var[:, prev:prev + 1],
                                           tmp[:, q + 1:q + 2], 1.0 / width,
                                           tmp[:, q + 2:q + 3], ALU.mult,
                                           ALU.subtract).then_inc(s_var, 1)
            for tt in range(n_tiles):
                emit_iter(tt, tt - 1 if tt >= 1 else None)
            emit_iter(None, n_tiles - 1)


def emit_bounce_bcast(P, src_1xk_ap, k, dst_sb):
    nc = P.nc
    scr = P.dscr(P.uname("bscr"), [1, k], F32)
    sd = P.sem()
    with nc.Block() as blk:
        @blk.sync
        def _(sy):
            sy = P.chain(sy)
            sy.dma_start(out=scr[0:1, :], in_=src_1xk_ap).then_inc(sd, 16)
            sy.wait_ge(sd, 16)
            src = bass.AP(scr.tensor if hasattr(scr, 'tensor') else scr, 0, [[0, 128], [1, k]])
            sy.dma_start(out=dst_sb[:, 0:k], in_=src).then_inc(sd, 16)
            sy.wait_ge(sd, 32)


def emit_partition_max(P, acc_128x1, out_1x1):
    nc = P.nc
    scr = P.dscr(P.uname("pmx"), [128, 1], F32)
    row = P.sb([1, 128], F32)
    sd = P.sem()
    with nc.Block() as blk:
        @blk.sync
        def _(sy):
            sy = P.chain(sy)
            sy.dma_start(out=scr[:, :], in_=acc_128x1[:, 0:1]).then_inc(sd, 16)
            sy.wait_ge(sd, 16)
            rap = bass.AP(scr.tensor, 0, [[0, 1], [1, 128]])
            sy.dma_start(out=row[0:1, :], in_=rap).then_inc(sd, 16)
            sy.wait_ge(sd, 32)
    with nc.Block() as blk:
        @blk.vector
        def _(v):
            v = P.chain(v)
            v.reduce_max(out_1x1[0:1, 0:1], row[0:1, 0:128], axis=AX)


def sb_ap3(t, col0, d1_stride, d1_n, d2_n):
    return bass.AP(t, col0,
                   [[t.shape[1], 128], [d1_stride, d1_n], [1, d2_n]])


def build_L1():
    P = Prog()
    nc = P.nc
    xs = P.din("xs", [S, D])
    m_out = P.dout("m1", [1, 1])

    x_sb = P.sb([128, NT * D], F32, "x_sb")
    mean = P.sb([128, NT], F32)
    rstd = P.sb([128, NT], F32)
    acc = P.sb([128, 1], F32)
    tmp1 = P.sb([128, 2], F32)
    xl = P.sb([128, 2 * D], F32)
    m1 = P.sb([1, 1], F32)
    sd = P.sem("sd")

    with nc.Block() as blk:
        @blk.sync
        def _(sy):
            sy = P.chain(sy)
            sy.dma_start(out=x_sb[:, :],
                         in_=xs.rearrange("(t p) d -> p t d", p=128)).then_inc(sd, 16)
            sy.wait_ge(sd, 16)

    emit_ln_stats(P, x_sb, NT, D, mean, rstd)

    with nc.Block() as blk:
        @blk.vector
        def _(v):
            v = P.chain(v)
            v.memset(acc[:, :], 0.0)
            for tt in range(NT + 2):
                if tt < NT:
                    xa = x_sb[:, tt * D:(tt + 1) * D]
                    v.tensor_scalar(xl[:, (tt % 2) * D:(tt % 2 + 1) * D], xa,
                                    mean[:, tt:tt + 1], rstd[:, tt:tt + 1],
                                    ALU.subtract, ALU.mult)
                if 1 <= tt <= NT:
                    u = tt - 1
                    v.tensor_reduce(tmp1[:, u % 2:u % 2 + 1],
                                    xl[:, (u % 2) * D:(u % 2 + 1) * D], axis=AX,
                                    op=ALU.max, apply_absolute_value=True)
                if tt >= 2:
                    w = tt - 2
                    v.tensor_max(acc[:, 0:1], acc[:, 0:1],
                                 tmp1[:, w % 2:w % 2 + 1])

    emit_partition_max(P, acc, m1)

    with nc.Block() as blk:
        @blk.sync
        def _(sy):
            sy = P.chain(sy)
            sy.dma_start(out=m_out[0:1, 0:1], in_=m1[0:1, 0:1]).then_inc(sd, 16)
            sy.wait_ge(sd, 32)
    return P.close()


def emit_gscale(P, M8, clipin, sc, clp, scb, extra_half=True):
    nc = P.nc
    sd = P.sem()
    with nc.Block() as blk:
        @blk.sync
        def _(sy):
            sy = P.chain(sy)
            sy.dma_start(out=sc[0:1, 0:8], in_=M8[0:1, :]).then_inc(sd, 16)
            sy.dma_start(out=clp[0:1, 0:1], in_=clipin[0:1, 0:1]).then_inc(sd, 16)
            sy.wait_ge(sd, 32)
    with nc.Block() as blk:
        @blk.vector
        def _(v):
            v = P.chain(v)
            v.reduce_max(sc[0:1, 0:1], sc[0:1, 0:8], axis=AX)
    with nc.Block() as blk:
        @blk.scalar
        def _(s):
            s = P.chain(s)
            s.activation(sc[0:1, 1:2], clp[0:1, 0:1], AF.Sigmoid)
    with nc.Block() as blk:
        @blk.vector
        def _(v):
            v = P.chain(v)
            v.scalar_tensor_tensor(sc[0:1, 2:3], sc[0:1, 0:1], 1.0 / 127.0,
                                   sc[0:1, 1:2], ALU.mult, ALU.mult)
            v.reciprocal(sc[0:1, 3:4], sc[0:1, 2:3])
            v.tensor_scalar_mul(sc[0:1, 4:5], sc[0:1, 2:3], 0.5)
    emit_bounce_bcast(P, sc[0:1, 2:5], 3, scb)


def build_L2():
    P = Prog()
    nc = P.nc
    xs = P.din("xs", [S, D])
    M8 = P.din("m8", [1, 8])
    clip_x = P.din("clip_x", [1, 1])
    wg = P.din("wg", [DI, D])
    wu = P.din("wu", [DI, D])
    h_out = P.dout("h", [S, DI])
    m_out = P.dout("m2", [1, 1])

    emit_ident(P)

    sc = P.sb([1, 8], F32)
    clp = P.sb([1, 1], F32)
    scb = P.sb([128, 3], F32)
    rr = P.sb([128, NT], F32)
    ri = P.sb([128, D], I32)
    rf = P.sb([128, D], F32)
    q_bf = P.sb([128, D], BF16)
    xqT = P.sb([128, 8 * S], BF16, "xqT")
    sg_sb = P.sb([128, DI], F32)
    h_sb = P.sb([128, 2 * DI], F32)
    hacc = P.sb([128, 1], F32)
    tmp1 = P.sb([128, 4], F32)
    m2 = P.sb([1, 1], F32)
    wtile = P.sb([128, 4 * D], F32, "wtile2")
    P.push_region()
    x_sb = P.sb([128, NT * D], F32, "x_sb")
    mean = P.sb([128, NT], F32)
    rstd = P.sb([128, NT], F32)

    ps_tp = P.ps([128, 1024], BF16, "ps_tp")
    f_chunks = chunks_of(DI, 512)
    NF = len(f_chunks)
    ps_mm = [P.ps([128, 512], F32, f"psmm{i}") for i in range(NF)]

    sd = P.sem("sd")
    ds_w = P.dsem(4, 1, "dsw")
    ds_h = P.dsem(2, 1, "dsh")
    s_q = P.sem("s_q")
    s_tp = P.sem("s_tp")
    s_ev = P.sem("s_ev")
    s_mm = P.sem("s_mm")
    s_sg = P.sem("s_sg")
    s_hv = P.sem("s_hv")

    with nc.Block() as blk:
        @blk.sync
        def _(sy):
            sy = P.chain(sy)
            sy.dma_start(out=x_sb[:, :],
                         in_=xs.rearrange("(t p) d -> p t d", p=128)).then_inc(sd, 16)
            sy.wait_ge(sd, 16)

    emit_ln_stats(P, x_sb, NT, D, mean, rstd)
    emit_gscale(P, M8, clip_x, sc, clp, scb)

    NRW = DI // 128

    with nc.Block() as blk:
        @blk.gpsimd
        def _(g):
            g = P.chain(g)
            for k in range(4):
                g.dma_start(out=wtile[:, k * D:(k + 1) * D],
                            in_=wg[128 * k:128 * (k + 1), :]
                            ).then_inc(ds_w.sem(k), 16)

        @blk.vector
        def _(v):
            v = P.chain(v)
            v.tensor_scalar(rr[:, 0:NT], rstd[:, 0:NT], scb[:, 1:2], None,
                            ALU.mult)
            for tt in range(NT):
                if tt >= 1:
                    v.wait_ge(s_tp, tt)
                v.tensor_scalar(ri[:, :],
                                x_sb[:, tt * D:(tt + 1) * D],
                                mean[:, tt:tt + 1], rr[:, tt:tt + 1],
                                ALU.subtract, ALU.mult)
                v.tensor_scalar(q_bf[:, :], ri[:, :],
                                127.0, -128.0, ALU.min,
                                ALU.max).then_inc(s_q, 1)

        @blk.tensor
        def _(t):
            t = P.chain(t)
            for k in range(NT):
                t.wait_ge(s_q, k + 1)
                if k >= 1:
                    t.wait_ge(s_ev, k)
                for j in range(8):
                    inst = t.transpose(ps_tp[:, 128 * j:128 * (j + 1)],
                                       q_bf[:, 128 * j:128 * (j + 1)],
                                       P.ident[:, :])
                inst.then_inc(s_tp, 1)

        @blk.scalar
        def _(s):
            s = P.chain(s)
            for k in range(NT):
                s.wait_ge(s_tp, k + 1)
                dst = sb_ap3(xqT, 128 * k, S, 8, 128)
                src = sb_ap3(ps_tp, 0, 128, 8, 128)
                s.copy(dst, src).then_inc(s_ev, 1)

    P.pop_region()
    wrow = P.sb([128, 8], F32)
    wgT = P.sb([128, 8 * DI], BF16, "wgT")
    wuT = P.sb([128, 8 * DI], BF16, "wuT")

    with nc.Block() as blk:
        @blk.gpsimd
        def _(g):
            g = P.chain(g)
            for k in range(4, 2 * NRW):
                wi, r = divmod(k, NRW)
                wsrc = wg if wi == 0 else wu
                g.wait_ge(s_q, NT + k - 3)
                g.dma_start(out=wtile[:, (k % 4) * D:(k % 4 + 1) * D],
                            in_=wsrc[128 * r:128 * (r + 1), :]
                            ).then_inc(ds_w.sem(k), 16)

        @blk.vector
        def _(v):
            v = P.chain(v)
            for k in range(2 * NRW):
                ds_w.wait(v, k)
                v.wait_ge(s_tp, NT + k)
                wt = wtile[:, (k % 4) * D:(k % 4 + 1) * D]
                v.tensor_reduce(wrow[:, 0:1], wt, axis=AX, op=ALU.max,
                                apply_absolute_value=True)
                v.tensor_scalar_mul(wrow[:, 3:4], wrow[:, 0:1], 1.0 / 127.0)
                v.reciprocal(wrow[:, 1:2], wrow[:, 0:1])
                v.tensor_scalar_mul(wrow[:, 2:3], wrow[:, 1:2], 127.0)
                v.tensor_scalar(ri[:, :], wt, wrow[:, 2:3], None, ALU.mult)
                v.tensor_scalar(q_bf[:, :], ri[:, :],
                                wrow[:, 3:4], None,
                                ALU.mult).then_inc(s_q, 1)

        @blk.tensor
        def _(t):
            t = P.chain(t)
            for k in range(2 * NRW):
                t.wait_ge(s_q, NT + k + 1)
                t.wait_ge(s_ev, NT + k)
                for j in range(8):
                    inst = t.transpose(ps_tp[:, 128 * j:128 * (j + 1)],
                                       q_bf[:, 128 * j:128 * (j + 1)],
                                       P.ident[:, :])
                inst.then_inc(s_tp, 1)

        @blk.scalar
        def _(s):
            s = P.chain(s)
            for k in range(2 * NRW):
                s.wait_ge(s_tp, NT + k + 1)
                wi, r = divmod(k, NRW)
                dstT = wgT if wi == 0 else wuT
                dst = sb_ap3(dstT, 128 * r, DI, 8, 128)
                src = sb_ap3(ps_tp, 0, 128, 8, 128)
                s.copy(dst, src).then_inc(s_ev, 1)

    with nc.Block() as blk:
        @blk.tensor
        def _(t):
            t = P.chain(t)
            for tt in range(NT):
                for fi, (f0, fw) in enumerate(f_chunks):
                    if tt >= 1:
                        t.wait_ge(s_hv, (tt - 1) * NF + fi + 1)
                    for ci in range(8):
                        inst = t.matmul(
                            ps_mm[fi][:, 0:fw],
                            xqT[:, ci * S + 128 * tt: ci * S + 128 * tt + 128],
                            wgT[:, ci * DI + f0: ci * DI + f0 + fw],
                            start=(ci == 0), stop=(ci == 7))
                    inst.then_inc(s_mm, 1)
                for fi, (f0, fw) in enumerate(f_chunks):
                    t.wait_ge(s_sg, tt * NF + fi + 1)
                    for ci in range(8):
                        inst = t.matmul(
                            ps_mm[fi][:, 0:fw],
                            xqT[:, ci * S + 128 * tt: ci * S + 128 * tt + 128],
                            wuT[:, ci * DI + f0: ci * DI + f0 + fw],
                            start=(ci == 0), stop=(ci == 7))
                    inst.then_inc(s_mm, 1)

        @blk.scalar
        def _(s):
            s = P.chain(s)
            for tt in range(NT):
                for fi, (f0, fw) in enumerate(f_chunks):
                    s.wait_ge(s_mm, 2 * NF * tt + fi + 1)
                    if tt >= 1:
                        s.wait_ge(s_hv, (tt - 1) * NF + fi + 1)
                    s.activation(sg_sb[:, f0:f0 + fw], ps_mm[fi][:, 0:fw],
                                 AF.Silu, scale=scb[:, 0:1]).then_inc(s_sg, 1)

        @blk.vector
        def _(v):
            v = P.chain(v)
            v.memset(hacc[:, :], 0.0)
            for tt in range(NT):
                hb = (tt % 2) * DI
                for fi, (f0, fw) in enumerate(f_chunks):
                    v.wait_ge(s_mm, 2 * NF * tt + NF + fi + 1)
                    v.wait_ge(s_sg, tt * NF + fi + 1)
                    if tt >= 2:
                        ds_h.wait(v, tt - 2)
                    v.scalar_tensor_tensor(h_sb[:, hb + f0:hb + f0 + fw],
                                           ps_mm[fi][:, 0:fw], scb[:, 0:1],
                                           sg_sb[:, f0:f0 + fw],
                                           ALU.mult, ALU.mult).then_inc(s_hv, 1)
                v.tensor_reduce(tmp1[:, tt % 2:tt % 2 + 1], h_sb[:, hb:hb + DI],
                                axis=AX, op=ALU.max, apply_absolute_value=True)
                if tt >= 1:
                    v.tensor_max(hacc[:, 0:1], hacc[:, 0:1],
                                 tmp1[:, (tt - 1) % 2:(tt - 1) % 2 + 1])
            v.tensor_max(hacc[:, 0:1], hacc[:, 0:1],
                         tmp1[:, (NT - 1) % 2:(NT - 1) % 2 + 1])

        @blk.sync
        def _(sy):
            sy = P.chain(sy)
            for tt in range(NT):
                sy.wait_ge(s_hv, (tt + 1) * NF)
                hb = (tt % 2) * DI
                sy.dma_start(out=h_out[128 * tt:128 * (tt + 1), :],
                             in_=h_sb[:, hb:hb + DI]).then_inc(ds_h.sem(tt), 16)

    emit_partition_max(P, hacc, m2)

    with nc.Block() as blk:
        @blk.sync
        def _(sy):
            sy = P.chain(sy)
            sy.dma_start(out=m_out[0:1, 0:1], in_=m2[0:1, 0:1]).then_inc(sd, 16)
            sy.wait_ge(sd, 32)
            ds_h.wait_all(sy, NT)
    return P.close()


def build_L3():
    P = Prog()
    nc = P.nc
    xs = P.din("xs", [S, D])
    h_in = P.din("h", [S, DI])
    M8 = P.din("m8", [1, 8])
    clip_d = P.din("clip_d", [1, 1])
    wd = P.din("wd", [D, DI])
    x1_out = P.dout("x1", [S, D])

    emit_ident(P)
    NCI = DI // 128

    x_sb = P.sb([128, NT * D], F32, "x_sb")
    sc = P.sb([1, 8], F32)
    clp = P.sb([1, 1], F32)
    scb = P.sb([128, 3], F32)
    htile = P.sb([128, 2 * DI], F32)
    ri = P.sb([128, D], I32)
    rf = P.sb([128, D], F32)
    wrow = P.sb([128, 8], F32)
    q_bf = P.sb([128, DI], BF16)
    hqT = P.sb([128, 2 * NCI * 128], BF16)
    wdT = P.sb([128, NCI * D], BF16, "wdT")
    x1_sb = P.sb([128, 2 * D], F32)

    ps_tp = P.ps([128, NCI * 128], BF16, "ps_tp")
    f_chunks = chunks_of(D, 512)
    NF = len(f_chunks)
    ps_mm = [P.ps([128, 512], F32, f"psmm{i}") for i in range(NF)]

    sd = P.sem("sd")
    ds_t = P.dsem(2, 1, "dst")
    ds_x = P.dsem(2, 1, "dsx")
    s_q = P.sem("s_q")
    s_tp = P.sem("s_tp")
    s_ev = P.sem("s_ev")
    s_mm = P.sem("s_mm")
    s_xv = P.sem("s_xv")

    with nc.Block() as blk:
        @blk.sync
        def _(sy):
            sy = P.chain(sy)
            sy.dma_start(out=x_sb[:, :],
                         in_=xs.rearrange("(t p) d -> p t d", p=128)).then_inc(sd, 16)
            sy.wait_ge(sd, 16)

    emit_gscale(P, M8, clip_d, sc, clp, scb)

    NRW = D // 128
    with nc.Block() as blk:
        @blk.gpsimd
        def _(g):
            g = P.chain(g)
            for r in range(NRW):
                if r >= 2:
                    g.wait_ge(s_q, r - 1)
                g.dma_start(out=htile[:, (r % 2) * DI:(r % 2 + 1) * DI],
                            in_=wd[128 * r:128 * (r + 1), :]
                            ).then_inc(ds_t.sem(r), 16)

        @blk.vector
        def _(v):
            v = P.chain(v)
            for r in range(NRW):
                ds_t.wait(v, r)
                if r >= 1:
                    v.wait_ge(s_tp, r)
                wt = htile[:, (r % 2) * DI:(r % 2 + 1) * DI]
                v.tensor_reduce(wrow[:, 0:1], wt, axis=AX, op=ALU.max,
                                apply_absolute_value=True)
                v.tensor_scalar_mul(wrow[:, 3:4], wrow[:, 0:1], 1.0 / 127.0)
                v.reciprocal(wrow[:, 1:2], wrow[:, 0:1])
                v.tensor_scalar_mul(wrow[:, 2:3], wrow[:, 1:2], 127.0)
                inst = None
                for c0, cw in chunks_of(DI, D):
                    v.tensor_scalar(ri[:, 0:cw],
                                    htile[:, (r % 2) * DI + c0:
                                          (r % 2) * DI + c0 + cw],
                                    wrow[:, 2:3], None, ALU.mult)
                    inst = v.tensor_scalar(q_bf[:, c0:c0 + cw], ri[:, 0:cw],
                                           wrow[:, 3:4], None, ALU.mult)
                inst.then_inc(s_q, 1)

        @blk.tensor
        def _(t):
            t = P.chain(t)
            for r in range(NRW):
                t.wait_ge(s_q, r + 1)
                if r >= 1:
                    t.wait_ge(s_ev, r)
                for j in range(NCI):
                    inst = t.transpose(ps_tp[:, 128 * j:128 * (j + 1)],
                                       q_bf[:, 128 * j:128 * (j + 1)],
                                       P.ident[:, :])
                inst.then_inc(s_tp, 1)

        @blk.scalar
        def _(s):
            s = P.chain(s)
            for r in range(NRW):
                s.wait_ge(s_tp, r + 1)
                dst = sb_ap3(wdT, 128 * r, D, NCI, 128)
                src = sb_ap3(ps_tp, 0, 128, NCI, 128)
                s.copy(dst, src).then_inc(s_ev, 1)

    with nc.Block() as blk:
        @blk.gpsimd
        def _(g):
            g = P.chain(g)
            for tt in range(NT):
                if tt >= 2:
                    g.wait_ge(s_q, NRW + tt - 1)
                g.dma_start(out=htile[:, (tt % 2) * DI:(tt % 2 + 1) * DI],
                            in_=h_in[128 * tt:128 * (tt + 1), :]
                            ).then_inc(ds_t.sem(NRW + tt), 16)

        @blk.vector
        def _(v):
            v = P.chain(v)

            def qh(k):
                ds_t.wait(v, NRW + k)
                if k >= 1:
                    v.wait_ge(s_tp, NRW + k)
                hb = (k % 2) * DI
                inst = None
                for c0, cw in chunks_of(DI, D):
                    v.tensor_scalar(ri[:, 0:cw],
                                    htile[:, hb + c0:hb + c0 + cw],
                                    scb[:, 1:2], None, ALU.mult)
                    inst = v.tensor_scalar(q_bf[:, c0:c0 + cw], ri[:, 0:cw],
                                           127.0, -128.0, ALU.min, ALU.max)
                inst.then_inc(s_q, 1)

            qh(0)
            if NT > 1:
                qh(1)
            for tt in range(NT):
                xb = (tt % 2) * D
                for fi, (f0, fw) in enumerate(f_chunks):
                    v.wait_ge(s_mm, NF * tt + fi + 1)
                    if tt >= 2:
                        ds_x.wait(v, tt - 2)
                    v.scalar_tensor_tensor(x1_sb[:, xb + f0:xb + f0 + fw],
                                           ps_mm[fi][:, 0:fw], scb[:, 2:3],
                                           x_sb[:, tt * D + f0:tt * D + f0 + fw],
                                           ALU.mult, ALU.add).then_inc(s_xv, 1)
                if tt + 2 < NT:
                    qh(tt + 2)

        @blk.tensor
        def _(t):
            t = P.chain(t)
            for tt in range(NT):
                t.wait_ge(s_q, NRW + tt + 1)
                if tt >= 2:
                    t.wait_ge(s_ev, NRW + tt - 1)
                qb = (tt % 2) * NCI * 128
                for j in range(NCI):
                    inst = t.transpose(ps_tp[:, 128 * j:128 * (j + 1)],
                                       q_bf[:, 128 * j:128 * (j + 1)],
                                       P.ident[:, :])
                inst.then_inc(s_tp, 1)
                t.wait_ge(s_ev, NRW + tt + 1)
                for fi, (f0, fw) in enumerate(f_chunks):
                    if tt >= 1:
                        t.wait_ge(s_xv, (tt - 1) * NF + fi + 1)
                    for ci in range(NCI):
                        inst = t.matmul(
                            ps_mm[fi][:, 0:fw],
                            hqT[:, qb + 128 * ci: qb + 128 * ci + 128],
                            wdT[:, ci * D + f0: ci * D + f0 + fw],
                            start=(ci == 0), stop=(ci == NCI - 1))
                    inst.then_inc(s_mm, 1)

        @blk.scalar
        def _(s):
            s = P.chain(s)
            for tt in range(NT):
                s.wait_ge(s_tp, NRW + tt + 1)
                if tt >= 2:
                    s.wait_ge(s_mm, (tt - 1) * NF)
                qb = (tt % 2) * NCI * 128
                dst = bass.AP(hqT, qb, [[hqT.shape[1], 128], [128, NCI], [1, 128]])
                src = sb_ap3(ps_tp, 0, 128, NCI, 128)
                s.copy(dst, src).then_inc(s_ev, 1)

        @blk.sync
        def _(sy):
            sy = P.chain(sy)
            for tt in range(NT):
                sy.wait_ge(s_xv, (tt + 1) * NF)
                xb = (tt % 2) * D
                sy.dma_start(out=x1_out[128 * tt:128 * (tt + 1), :],
                             in_=x1_sb[:, xb:xb + D]).then_inc(ds_x.sem(tt), 16)
            ds_x.wait_all(sy, NT)
    return P.close()



DM = 2048
SP = S + 128
NTH = SP // 128


def build_L4():
    P = Prog()
    nc = P.nc
    x1h = P.din("x1h", [SP, D])
    w_inT = P.din("w_inT", [D, 2 * DM])
    w_xprojT = P.din("w_xprojT", [DM, 96])
    w_dtT = P.din("w_dtT", [64, DM])
    w_outT = P.din("w_outT", [DM, D])
    aux = P.din("aux", [128, 112])

    deltaT_o = P.dout("deltaT", [DM, S])
    duT_o = P.dout("duT", [DM, S])
    BC_o = P.dout("BC", [32, S])
    szT_o = P.dout("szT", [DM, S])
    part2_o = P.dout("part2", [S, D])

    uscr = P.dscr("uscr", [DM, S])

    emit_ident(P)
    one_t = P.constf(1.0)
    aux_sb = P.sb([128, 112], F32, "aux_sb")
    dbl_bf = P.sb([64, S], BF16, "dbl_bf")
    term2 = P.sb([128, 16 * S], BF16, "term2")
    P._term2 = term2
    ps_tp = P.ps([128, 1024], BF16, "ps_tp")

    P.push_region()
    x_lnT = P.sb([128, 8 * SP], BF16, "x_lnT")

    s_ld = P.sem("s_ld")
    s_ldg = P.sem("s_ldg")
    s_q = P.sem("s_q")
    s_tp = P.sem("s_tp")
    s_ev = P.sem("s_ev")

    P.push_region()
    x_sb = P.sb([128, NTH * D], F32, "x_sb")
    mean = P.sb([128, NTH], F32)
    rstd = P.sb([128, NTH], F32)
    q_bf = P.sb([128, D], BF16)

    with nc.Block() as blk:
        @blk.sync
        def _(sy):
            sy = P.chain(sy)
            sy.dma_start(out=x_sb[:, :],
                         in_=x1h.rearrange("(t p) d -> p t d", p=128)).then_inc(s_ld, 16)
            sy.wait_ge(s_ld, 16)

    emit_ln_stats(P, x_sb, NTH, D, mean, rstd)

    with nc.Block() as blk:
        @blk.vector
        def _(v):
            v = P.chain(v)
            for tt in range(NTH):
                if tt >= 1:
                    v.wait_ge(s_tp, tt)
                v.tensor_scalar(q_bf[:, :], x_sb[:, tt * D:(tt + 1) * D],
                                mean[:, tt:tt + 1], rstd[:, tt:tt + 1],
                                ALU.subtract, ALU.mult).then_inc(s_q, 1)

        @blk.tensor
        def _(t):
            t = P.chain(t)
            for tt in range(NTH):
                t.wait_ge(s_q, tt + 1)
                if tt >= 1:
                    t.wait_ge(s_ev, tt)
                for j in range(8):
                    inst = t.transpose(ps_tp[:, 128 * j:128 * (j + 1)],
                                       q_bf[:, 128 * j:128 * (j + 1)],
                                       P.ident[:, :])
                inst.then_inc(s_tp, 1)

        @blk.scalar
        def _(s):
            s = P.chain(s)
            for tt in range(NTH):
                s.wait_ge(s_tp, tt + 1)
                s.copy(sb_ap3(x_lnT, 128 * tt, SP, 8, 128),
                       sb_ap3(ps_tp, 0, 128, 8, 128)).then_inc(s_ev, 1)
    P.pop_region()

    P.push_region()
    w_str = P.sb([128, 4 * 1024], BF16, "w_str")
    wx_bf = P.sb([128, 16 * 96], BF16, "wx_bf")
    u_sb = P.sb([128, SP], F32, "u_sb")
    uc_sb = P.sb([128, S], F32, "uc_sb")
    u2_sb = P.sb([128, S], F32, "u2_sb")
    u2bf = P.sb([128, S], BF16, "u2bf")
    sz_sb = P.sb([128, S], F32, "sz_sb")
    dbl_sb = P.sb([96, S], F32, "dbl_sb")
    ps_xz = P.ps([128, 1024], F32, "ps_xz")
    ps_dbl = P.ps([96, S], F32, "ps_dbl")

    s_wld = P.sem("s_wld")
    ds_ws = P.dsem(2, 1, "dsws")
    s_xz = P.sem("s_xz")
    s_xe = P.sem("s_xe")
    s_cv = P.sem("s_cv")
    s_u2 = P.sem("s_u2")
    s_ub = P.sem("s_ub")
    ds_ud = P.dsem(2, 1, "dsud")
    s_sz = P.sem("s_sz")
    ds_sz = P.dsem(2, 1, "dssz")
    s_db = P.sem("s_db")

    tch = chunks_of(SP, 512)
    NTC = len(tch)

    with nc.Block() as blk:
        @blk.gpsimd
        def _(g):
            g = P.chain(g)
            g.dma_start(out=wx_bf[:, :],
                        in_=w_xprojT.rearrange("(c p) f -> p c f", p=128)
                        ).then_inc(s_wld, 16)
            g.dma_start(out=aux_sb[:, :], in_=aux[:, :]).then_inc(s_wld, 16)
            nw = 0
            for j in range(16):
                for co in (j, j + 16):
                    if nw >= 2:
                        g.wait_ge(s_xz, (nw - 1) * NTC)
                    wslice = w_inT[:, 128 * co:128 * (co + 1)]
                    g.dma_start(out=bass.AP(w_str, (nw % 2) * 2048,
                                            [[4 * 1024, 128], [128, 8], [1, 128]]),
                                in_=wslice.rearrange("(c p) f -> p c f", p=128)
                                ).then_inc(ds_ws.sem(nw), 16)
                    nw += 1

        @blk.tensor
        def _(t):
            t = P.chain(t)
            n_xz = 0
            nw = 0
            for j in range(16):
                for zi, co in enumerate((j, j + 16)):
                    if nw == 0:
                        t.wait_ge(s_wld, 32)
                    ds_ws.wait(t, nw)
                    for tc, (t0, twd) in enumerate(tch):
                        pb = (n_xz % 2) * 512
                        if n_xz >= 2:
                            t.wait_ge(s_xe, n_xz - 1)
                        for ci in range(8):
                            inst = t.matmul(
                                ps_xz[:, pb:pb + twd],
                                w_str[:, (nw % 2) * 2048 + 128 * ci:
                                      (nw % 2) * 2048 + 128 * ci + 128],
                                x_lnT[:, ci * SP + t0: ci * SP + t0 + twd],
                                start=(ci == 0), stop=(ci == 7))
                        inst.then_inc(s_xz, 1)
                        n_xz += 1
                    nw += 1
                    if zi == 1:
                        t.wait_ge(s_ub, 2 * j + 1)
                        for tc4, (t0, twd) in enumerate(chunks_of(S, 512)):
                            inst = t.matmul(
                                ps_dbl[:, t0:t0 + twd],
                                wx_bf[:, 96 * j:96 * (j + 1)],
                                u2bf[:, t0:t0 + twd],
                                start=(j == 0), stop=(j == 15))
                        inst.then_inc(s_db, 1)

        @blk.scalar
        def _(s):
            s = P.chain(s)
            n_xz = 0
            for j in range(16):
                for zi, co in enumerate((j, j + 16)):
                    for tc, (t0, twd) in enumerate(tch):
                        pb = (n_xz % 2) * 512
                        s.wait_ge(s_xz, n_xz + 1)
                        if zi == 0:
                            if tc == 0 and j >= 1:
                                s.wait_ge(s_cv, j)
                            s.copy(u_sb[:, t0:t0 + twd],
                                   ps_xz[:, pb:pb + twd]).then_inc(s_xe, 1)
                        else:
                            if tc == 0 and j >= 1:
                                s.wait_ge(s_ub, 2 * j)
                                ds_sz.wait(s, j - 1)
                            lo = max(t0, 128)
                            s.activation(sz_sb[:, lo - 128:t0 + twd - 128],
                                         ps_xz[:, pb + lo - t0:pb + twd],
                                         AF.Silu).then_inc(s_xe, 1)
                        n_xz += 1
                    if zi == 0:
                        s.wait_ge(s_cv, j + 1)
                        if j >= 1:
                            s.wait_ge(s_ub, 2 * j)
                            ds_ud.wait(s, j - 1)
                        s.activation(u2_sb[:, :], uc_sb[:, :], AF.Silu,
                                     bias=aux_sb[:, 64 + j:64 + j + 1]
                                     ).then_inc(s_u2, 1)

        @blk.vector
        def _(v):
            v = P.chain(v)
            for j in range(16):
                v.wait_ge(s_xe, (2 * j) * NTC + NTC)
                if j >= 1:
                    v.wait_ge(s_u2, j)
                v.tensor_scalar(uc_sb[:, :], u_sb[:, 125:125 + S],
                                aux_sb[:, 4 * j:4 * j + 1], None, ALU.mult)
                for k in (1, 2, 3):
                    v.scalar_tensor_tensor(uc_sb[:, :], u_sb[:, 125 + k:125 + k + S],
                                           aux_sb[:, 4 * j + k:4 * j + k + 1],
                                           uc_sb[:, :], ALU.mult,
                                           ALU.add)
                v.tensor_copy(uc_sb[:, 0:1], uc_sb[:, 0:1]).then_inc(s_cv, 1)
                v.wait_ge(s_u2, j + 1)
                if j >= 1:
                    v.wait_ge(s_db, j)
                v.tensor_copy(u2bf[:, :], u2_sb[:, :]).then_inc(s_ub, 1)
                v.wait_ge(s_xe, (2 * j + 2) * NTC)
                v.scalar_tensor_tensor(term2[:, j * S:(j + 1) * S],
                                       u2_sb[:, :], aux_sb[:, 80 + j:80 + j + 1],
                                       sz_sb[:, :], ALU.mult,
                                       ALU.mult).then_inc(s_ub, 1)

        @blk.sync
        def _(sy):
            sy = P.chain(sy)
            for j in range(16):
                sy.wait_ge(s_ub, 2 * j + 1)
                sy.dma_start(out=uscr[128 * j:128 * (j + 1), :],
                             in_=u2_sb[:, :]).then_inc(ds_ud.sem(j), 16)
                sy.wait_ge(s_ub, 2 * j + 2)
                sy.dma_start(out=szT_o[128 * j:128 * (j + 1), :],
                             in_=sz_sb[:, :]).then_inc(ds_sz.sem(j), 16)

    with nc.Block() as blk:
        @blk.scalar
        def _(s):
            s = P.chain(s)
            s.copy(dbl_sb[:, :], ps_dbl[:, :])

    with nc.Block() as blk:
        @blk.vector
        def _(v):
            v = P.chain(v)
            v.tensor_copy(dbl_bf[:, :], dbl_sb[0:64, :])

        @blk.sync
        def _(sy):
            sy = P.chain(sy)
            sy.dma_start(out=BC_o[:, :], in_=dbl_sb[64:96, :]).then_inc(s_ld, 16)

    with nc.Block() as blk:
        @blk.sync
        def _(sy):
            sy = P.chain(sy)
            sy.wait_ge(s_ld, 32)
            ds_sz.wait_all(sy, 16)
            ds_ud.wait_all(sy, 16)

    P.pop_region()
    P.pop_region()

    P.push_region()
    ps_d = P.ps([128, S], F32, "ps_d")
    wdt_bf = P.sb([64, DM], BF16, "wdt_bf")
    a_sb = P.sb([128, S], F32)
    v_sb = P.sb([128, S], F32)
    e_sb = P.sb([128, S], F32)
    l_sb = P.sb([128, S], F32)
    r_sb = P.sb([128, S], F32)
    d_sb = P.sb([128, 2 * S], F32)
    u2r = P.sb([128, 2 * S], F32)
    du_sb = P.sb([128, 2 * S], F32)

    s_dm = P.sem("s_dm")
    s_sp = P.sem("s_sp")
    s_dv = P.sem("s_dv")
    ds_dd = P.dsem(2, 2, "dsdd")
    ds_ur = P.dsem(2, 1, "dsur")

    with nc.Block() as blk:
        @blk.gpsimd
        def _(g):
            g = P.chain(g)
            g.dma_start(out=wdt_bf[:, :], in_=w_dtT[:, :]).then_inc(s_ldg, 16)
            g.wait_ge(s_ld, 32)
            g.wait_ge(s_ldg, 16)

    with nc.Block() as blk:
        @blk.gpsimd
        def _(g):
            g = P.chain(g)
            ds_ud.wait_all(g, 16)
            for do in range(16):
                if do >= 2:
                    g.wait_ge(s_dv, do - 1)
                g.dma_start(out=u2r[:, (do % 2) * S:(do % 2 + 1) * S],
                            in_=uscr[128 * do:128 * (do + 1), :]
                            ).then_inc(ds_ur.sem(do), 16)

        @blk.tensor
        def _(t):
            t = P.chain(t)
            for do in range(16):
                if do >= 1:
                    t.wait_ge(s_sp, 2 * do - 1)
                for tc, (t0, twd) in enumerate(chunks_of(S, 512)):
                    inst = t.matmul(ps_d[:, t0:t0 + twd],
                                    wdt_bf[:, 128 * do:128 * (do + 1)],
                                    dbl_bf[:, t0:t0 + twd],
                                    start=True, stop=True)
                inst.then_inc(s_dm, 1)

        @blk.scalar
        def _(s):
            s = P.chain(s)
            H = S // 2
            for do in range(16):
                s.wait_ge(s_dm, do + 1)
                if do >= 1:
                    s.wait_ge(s_dv, do)
                bcol = aux_sb[:, 96 + do:96 + do + 1]
                for hb in (0, H):
                    s.activation(a_sb[:, hb:hb + H], ps_d[:, hb:hb + H],
                                 AF.Abs, bias=bcol)
                inst = None
                for hb in (0, H):
                    inst = s.activation(v_sb[:, hb:hb + H], ps_d[:, hb:hb + H],
                                        AF.Identity, bias=bcol)
                inst.then_inc(s_sp, 1)
                inst = None
                for hb in (0, H):
                    inst = s.activation(e_sb[:, hb:hb + H], a_sb[:, hb:hb + H],
                                        AF.Exp, scale=-1.0)
                for hb in (0, H):
                    inst = s.activation(l_sb[:, hb:hb + H], e_sb[:, hb:hb + H],
                                        AF.Ln, bias=one_t[:, 0:1])
                inst.then_inc(s_sp, 1)

        @blk.vector
        def _(v):
            v = P.chain(v)
            H = S // 2
            for do in range(16):
                v.wait_ge(s_sp, 2 * do + 2)
                db = (do % 2) * S
                for hb in (0, H):
                    v.tensor_scalar(r_sb[:, hb:hb + H], v_sb[:, hb:hb + H],
                                    0.0, None, ALU.max)
                if do >= 2:
                    ds_dd.wait(v, do - 2)
                inst = None
                for hb in (0, H):
                    inst = v.tensor_add(d_sb[:, db + hb:db + hb + H],
                                        r_sb[:, hb:hb + H], l_sb[:, hb:hb + H])
                ds_ur.wait(v, do)
                for hb in (0, H):
                    inst = v.tensor_mul(du_sb[:, db + hb:db + hb + H],
                                        d_sb[:, db + hb:db + hb + H],
                                        u2r[:, db + hb:db + hb + H])
                inst.then_inc(s_dv, 1)

        @blk.sync
        def _(sy):
            sy = P.chain(sy)
            for do in range(16):
                sy.wait_ge(s_dv, do + 1)
                db = (do % 2) * S
                sy.dma_start(out=deltaT_o[128 * do:128 * (do + 1), :],
                             in_=d_sb[:, db:db + S]).then_inc(ds_dd.sem(do), 16)
                sy.dma_start(out=duT_o[128 * do:128 * (do + 1), :],
                             in_=du_sb[:, db:db + S]).then_inc(ds_dd.sem(do), 16)
    P.pop_region()

    P.push_region()
    ps_p2 = [P.ps([128, 512], F32, f"psp2{i}") for i in range(2)]
    wo_bf = P.sb([128, 16 * D], BF16, "wo_bf")
    p2_sb = P.sb([128, 2 * D], F32)

    s_wo = P.sem("s_wo")
    s_p2 = P.sem("s_p2")
    s_pv = P.sem("s_pv")
    ds_pd = P.dsem(2, 1, "dspd")

    with nc.Block() as blk:
        @blk.gpsimd
        def _(g):
            g = P.chain(g)
            g.dma_start(out=wo_bf[:, :],
                        in_=w_outT.rearrange("(c p) f -> p c f", p=128)
                        ).then_inc(s_wo, 16)
            g.wait_ge(s_wo, 16)

    with nc.Block() as blk:
        @blk.tensor
        def _(t):
            t = P.chain(t)
            for tt in range(NT):
                for fi, (f0, fw) in enumerate(chunks_of(D, 512)):
                    if tt >= 1:
                        t.wait_ge(s_pv, (tt - 1) * 2 + fi + 1)
                    for do in range(16):
                        inst = t.matmul(
                            ps_p2[fi][:, 0:fw],
                            term2[:, do * S + 128 * tt: do * S + 128 * tt + 128],
                            wo_bf[:, do * D + f0: do * D + f0 + fw],
                            start=(do == 0), stop=(do == 15))
                    inst.then_inc(s_p2, 1)

        @blk.vector
        def _(v):
            v = P.chain(v)
            for tt in range(NT):
                xb = (tt % 2) * D
                for fi, (f0, fw) in enumerate(chunks_of(D, 512)):
                    v.wait_ge(s_p2, tt * 2 + fi + 1)
                    if tt >= 2:
                        ds_pd.wait(v, tt - 2)
                    v.tensor_copy(p2_sb[:, xb + f0:xb + f0 + fw],
                                  ps_p2[fi][:, 0:fw]).then_inc(s_pv, 1)

        @blk.sync
        def _(sy):
            sy = P.chain(sy)
            for tt in range(NT):
                sy.wait_ge(s_pv, (tt + 1) * 2)
                xb = (tt % 2) * D
                sy.dma_start(out=part2_o[128 * tt:128 * (tt + 1), :],
                             in_=p2_sb[:, xb:xb + D]).then_inc(ds_pd.sem(tt), 16)
            ds_pd.wait_all(sy, NT)
            ds_dd.wait_all(sy, 16)
            ds_sz.wait_all(sy, 16)
            ds_ud.wait_all(sy, 16)
    P.pop_region()
    return P.close()



TC = 1024
NCH = 8192 // TC


def build_L5():
    P = Prog()
    nc = P.nc
    dS = P.din("dS", [256, 16384])
    duS = P.din("duS", [256, 16384])
    Bb = P.din("Bb", [2, 128, 8192])
    Cb = P.din("Cb", [2, 128, 8192])
    A_lane = P.din("A_lane", [128, 32])
    E_in = P.din("E", [128, 2048])
    ET_in = P.din("ET", [128, 2048])
    ys_o = P.dout("ys", [256, 16384])

    E_bf = P.sb([128, 2048], BF16, "E_bf")
    ET_bf = P.sb([128, 2048], BF16, "ET_bf")
    a_sb = P.sb([128, 32], F32, "a_sb")
    at_sb = P.sb([128, 32], F32)
    st_sb = P.sb([128, 64], F32, "st_sb")
    zero1 = P.sb([128, 1], F32, "zero1")
    Bb_ch = P.sb([128, 2 * TC], F32, "Bb_ch")
    Cb_ch = P.sb([128, 2 * TC], F32, "Cb_ch")
    dbf = P.sb([128, 2 * TC], BF16, "dbf")
    dubf = P.sb([128, 2 * TC], BF16, "dubf")
    dA = P.sb([128, 2 * TC], F32, "dA")
    dBu = P.sb([128, 2 * TC], F32, "dBu")
    h_sb = P.sb([128, 2 * TC], F32, "h_sb")
    hC = P.sb([128, 2 * TC], BF16, "hC")
    ys_sb = P.sb([128, 2 * TC], F32, "ys_sb")

    psA = P.ps([128, TC], F32, "psA")
    psB = P.ps([128, TC], F32, "psB")
    psY = [P.ps([128, TC], F32, f"psY{g}") for g in range(2)]

    s_init = P.sem("s_init")
    ds_bc = P.dsem(2, 2, "dsbc")
    ds_dd = P.dsem(2, 2, "dsdd")
    s_mmA = P.sem("s_mmA")
    s_evA = P.sem("s_evA")
    s_mmB = P.sem("s_mmB")
    s_evB = P.sem("s_evB")
    s_sc = P.sem("s_sc")
    s_hc = P.sem("s_hc")
    s_mmYd = P.sem("s_mmYd")
    s_evY = P.sem("s_evY")
    ds_yd = P.dsem(2, 1, "dsyd")

    with nc.Block() as blk:
        @blk.gpsimd
        def _(g):
            g = P.chain(g)
            g.memset(zero1[:, :], 0.0)
            g.dma_start(out=E_bf[:, :], in_=E_in[:, :]).then_inc(s_init, 16)
            g.dma_start(out=ET_bf[:, :], in_=ET_in[:, :]).then_inc(s_init, 16)
            g.dma_start(out=at_sb[:, :], in_=A_lane[:, :]).then_inc(s_init, 16)
            g.wait_ge(s_init, 48)
    with nc.Block() as blk:
        @blk.scalar
        def _(s):
            s = P.chain(s)
            s.activation(a_sb[:, :], at_sb[:, :], AF.Exp)
    with nc.Block() as blk:
        @blk.vector
        def _(v):
            v = P.chain(v)
            v.tensor_scalar_mul(a_sb[:, :], a_sb[:, :], -1.0)
            v.memset(st_sb[:, :], 0.0)

    def K(b, cc, g, db):
        return ((b * NCH + cc) * 2 + g) * 16 + db

    def KG(b, cc, g):
        return (b * NCH + cc) * 2 + g

    def KC(b, cc):
        return b * NCH + cc

    NKG = 2 * NCH * 2
    NK = NKG * 16

    with nc.Block() as blk:
        @blk.sync
        def _(sy):
            sy = P.chain(sy)
            for b in range(2):
                for cc in range(NCH):
                    kc = KC(b, cc)
                    pb = (kc % 2) * TC
                    if kc >= 2:
                        sy.wait_ge(s_evY, 2 * (kc - 1))
                    sy.dma_start(out=Bb_ch[:, pb:pb + TC],
                                 in_=Bb[b, :, cc * TC:(cc + 1) * TC]
                                 ).then_inc(ds_bc.sem(kc), 16)
                    sy.dma_start(out=Cb_ch[:, pb:pb + TC],
                                 in_=Cb[b, :, cc * TC:(cc + 1) * TC]
                                 ).then_inc(ds_bc.sem(kc), 16)
                    for g in range(2):
                        kg = KG(b, cc, g)
                        sy.wait_ge(s_evY, kg + 1)
                        sy.dma_start(
                            out=ys_o[128 * g:128 * (g + 1),
                                     b * 8192 + cc * TC: b * 8192 + (cc + 1) * TC],
                            in_=ys_sb[:, (kg % 2) * TC:(kg % 2 + 1) * TC]
                        ).then_inc(ds_yd.sem(kg), 16)
            ds_yd.wait_all(sy, NKG)

        @blk.gpsimd
        def _(g_):
            g_ = P.chain(g_)
            for b in range(2):
                for cc in range(NCH):
                    for g in range(2):
                        kg = KG(b, cc, g)
                        if kg >= 2:
                            g_.wait_ge(s_mmB, (kg - 1) * 16)
                        r0 = 128 * g
                        c0 = b * 8192 + cc * TC
                        pb = (kg % 2) * TC
                        g_.dma_start(out=dbf[:, pb:pb + TC],
                                     in_=dS[r0:r0 + 128, c0:c0 + TC]
                                     ).then_inc(ds_dd.sem(kg), 16)
                        g_.dma_start(out=dubf[:, pb:pb + TC],
                                     in_=duS[r0:r0 + 128, c0:c0 + TC]
                                     ).then_inc(ds_dd.sem(kg), 16)

        @blk.tensor
        def _(t):
            t = P.chain(t)

            def emit_psy(kk):
                db = kk % 16
                kg = kk // 16
                g = kg % 2
                if db == 0 and kg >= 2:
                    t.wait_ge(s_evY, kg - 1)
                t.wait_ge(s_hc, kk + 1)
                ett = ET_bf[:, 128 * db:128 * (db + 1)]
                for h2 in (0, 512):
                    inst = t.matmul(psY[g][:, h2:h2 + 512], ett,
                                    hC[:, (kk % 2) * TC + h2:
                                       (kk % 2) * TC + h2 + 512],
                                    start=(db == 0), stop=(db == 15))
                inst.then_inc(s_mmYd, 1)

            for k in range(NK):
                db = k % 16
                kg = k // 16
                if db == 0:
                    ds_dd.wait(t, kg)
                pb = (kg % 2) * TC
                est = E_bf[:, 128 * db:128 * (db + 1)]
                if k >= 1:
                    t.wait_ge(s_evA, k)
                for h2 in (0, 512):
                    inst = t.matmul(psA[:, h2:h2 + 512], est,
                                    dbf[:, pb + h2:pb + h2 + 512],
                                    start=True, stop=True)
                inst.then_inc(s_mmA, 1)
                if k >= 1:
                    t.wait_ge(s_evB, k)
                for h2 in (0, 512):
                    inst = t.matmul(psB[:, h2:h2 + 512], est,
                                    dubf[:, pb + h2:pb + h2 + 512],
                                    start=True, stop=True)
                inst.then_inc(s_mmB, 1)
                if k >= 1:
                    emit_psy(k - 1)
            emit_psy(NK - 1)

        @blk.scalar
        def _(s):
            s = P.chain(s)
            for b in range(2):
                for cc in range(NCH):
                    for g in range(2):
                        kg = KG(b, cc, g)
                        for db in range(16):
                            k = K(b, cc, g, db)
                            s.wait_ge(s_mmA, k + 1)
                            if k >= 2:
                                s.wait_ge(s_sc, k - 1)
                            acol = a_sb[:, g * 16 + db:g * 16 + db + 1]
                            s.activation(dA[:, (k % 2) * TC:(k % 2 + 1) * TC],
                                         psA[:, :], AF.Exp,
                                         scale=acol).then_inc(s_evA, 1)
                        s.wait_ge(s_mmYd, 16 * (kg + 1))
                        if kg >= 2:
                            ds_yd.wait(s, kg - 2)
                        s.copy(ys_sb[:, (kg % 2) * TC:(kg % 2 + 1) * TC],
                               psY[g][:, :]).then_inc(s_evY, 1)

        @blk.vector
        def _(v):
            v = P.chain(v)
            for b in range(2):
                for cc in range(NCH):
                    kc = KC(b, cc)
                    for g in range(2):
                        kg = KG(b, cc, g)
                        for db in range(16):
                            k = K(b, cc, g, db)
                            si = (b * 2 + g) * 16 + db
                            v.wait_ge(s_mmB, k + 1)
                            ds_bc.wait(v, kc)
                            v.tensor_tensor(dBu[:, (k % 2) * TC:(k % 2 + 1) * TC],
                                            psB[:, :],
                                            Bb_ch[:, (kc % 2) * TC:(kc % 2 + 1) * TC],
                                            ALU.mult).then_inc(s_evB, 1)
                            v.wait_ge(s_evA, k + 1)
                            init = (zero1[:, 0:1] if cc == 0
                                    else st_sb[:, si:si + 1])
                            v.tensor_tensor_scan(
                                h_sb[:, (k % 2) * TC:(k % 2 + 1) * TC],
                                dA[:, (k % 2) * TC:(k % 2 + 1) * TC],
                                dBu[:, (k % 2) * TC:(k % 2 + 1) * TC],
                                init, ALU.mult, ALU.add).then_inc(s_sc, 1)
                            if k >= 2:
                                v.wait_ge(s_mmYd, k - 1)
                            v.tensor_tensor(hC[:, (k % 2) * TC:(k % 2 + 1) * TC],
                                            h_sb[:, (k % 2) * TC:(k % 2 + 1) * TC],
                                            Cb_ch[:, (kc % 2) * TC:(kc % 2 + 1) * TC],
                                            ALU.mult).then_inc(s_hc, 1)
                            v.tensor_copy(st_sb[:, si:si + 1],
                                          h_sb[:, (k % 2) * TC + TC - 1:
                                               (k % 2) * TC + TC])
    return P.close()



DM = 2048
DF = 4096
EPS_RMS = 1e-6
EPS_LN = 1e-5
GC1 = 0.7978845608028654
GC2 = 0.044715


def emit_partition_reduce(P, acc_128x1, out_1x1, op):
    nc = P.nc
    scr = P.dscr(P.uname("prx"), [128, 1], F32)
    row = P.sb([1, 128], F32)
    sd = P.sem()
    with nc.Block() as blk:
        @blk.sync
        def _(sy):
            sy = P.chain(sy)
            sy.dma_start(out=scr[:, :], in_=acc_128x1[:, 0:1]).then_inc(sd, 16)
            sy.wait_ge(sd, 16)
            rap = bass.AP(scr.tensor, 0, [[0, 1], [1, 128]])
            sy.dma_start(out=row[0:1, :], in_=rap).then_inc(sd, 16)
            sy.wait_ge(sd, 32)
    with nc.Block() as blk:
        @blk.vector
        def _(v):
            v = P.chain(v)
            v.tensor_reduce(out_1x1[0:1, 0:1], row[0:1, 0:128], axis=AX, op=op)


def build_L6(phases="abc"):
    P = Prog()
    nc = P.nc
    x1 = P.din("x1", [S, D])
    ysT = P.din("ysT", [DM, S])
    szT = P.din("szT", [DM, S])
    part2 = P.din("part2", [S, D])
    w_outT = P.din("w_outT", [DM, D])
    n1w = P.din("n1w", [1, D])
    n2w = P.din("n2w", [1, D])
    w1 = P.din("w1", [DF, D])
    b1 = P.din("b1", [1, DF])
    w2 = P.din("w2", [D, DF])
    b2 = P.din("b2", [1, D])
    x2_o = P.dout("x2", [S, D])
    m3_o = P.dout("m3", [1, 1])

    x1b_scr = P.dscr("x1b_scr", [S, D])

    emit_ident(P)
    eps_r = P.constf(EPS_RMS)
    eps_l = P.constf(EPS_LN)

    n1b = P.sb([128, D], F32, "n1b")
    n2b = P.sb([128, D], F32, "n2b")
    b2b = P.sb([128, D], F32, "b2b")
    s_ld = P.sem("s_ld")

    with nc.Block() as blk:
        @blk.sync
        def _(sy):
            sy = P.chain(sy)
            sy.dma_start(out=n1b[:, :],
                         in_=bass.AP(n1w.tensor, 0, [[0, 128], [1, D]])
                         ).then_inc(s_ld, 16)
            sy.dma_start(out=n2b[:, :],
                         in_=bass.AP(n2w.tensor, 0, [[0, 128], [1, D]])
                         ).then_inc(s_ld, 16)
            sy.dma_start(out=b2b[:, :],
                         in_=bass.AP(b2.tensor, 0, [[0, 128], [1, D]])
                         ).then_inc(s_ld, 16)
            sy.wait_ge(s_ld, 48)

    mean = P.sb([128, NT], F32, "l6mean")
    rstd = P.sb([128, NT], F32, "l6rstd")
    P.push_region()
    P.push_region()
    x_sb = P.sb([128, NT * D], F32, "x_sb")
    s_x0 = P.sem("s_x0")
    with nc.Block() as blk:
        @blk.sync
        def _(sy):
            sy = P.chain(sy)
            sy.dma_start(out=x_sb[:, :],
                         in_=x1.rearrange("(t p) d -> p t d", p=128)
                         ).then_inc(s_x0, 16)
            sy.wait_ge(s_x0, 16)
    emit_ln_stats(P, x_sb, NT, D, mean, rstd)
    P.pop_region()

    if phases == "ln":
        with nc.Block() as blk:
            @blk.sync
            def _(sy):
                sy = P.chain(sy)
                sy.dma_start(out=x2_o[:, :], in_=x1[:, :]).then_inc(s_ld, 16)
                sy.dma_start(out=m3_o[0:1, 0:1], in_=mean[0:1, 0:1]
                             ).then_inc(s_ld, 16)
                sy.wait_ge(s_ld, 80)
        while P._regions:
            P.pop_region()
        return P.close()

    y1bf = P.sb([128, 16 * S], BF16, "y1bf")
    wo_bf = P.sb([128, 16 * D], BF16, "wo_bf")
    yrow = P.sb([128, 2 * S], F32)
    srow = P.sb([128, 2 * S], F32)
    x1t = P.sb([128, 2 * D], F32)
    p2t = P.sb([128, 2 * D], F32)
    hres = P.sb([128, D], F32)
    scr1 = P.sb([128, D], F32)
    rms_t = P.sb([128, 6], F32)
    x1b_sb = P.sb([128, 2 * D], F32)
    ps_mo = [P.ps([128, 512], F32, f"psmo{i}") for i in range(2)]

    s_yd = P.sem("s_yd")
    s_y1 = P.sem("s_y1")
    ds_p2 = P.dsem(2, 2, "dsp2")
    s_mm = P.sem("s_mm")
    s_hv = P.sem("s_hv")
    ds_xb = P.dsem(2, 1, "dsxb")
    s_rs = P.sem("s_rs")

    ds_y = P.dsem(2, 1, "dsy6")
    ds_s = P.dsem(2, 1, "dss6")
    with nc.Block() as blk:
        @blk.gpsimd
        def _(g):
            g = P.chain(g)
            g.dma_start(out=wo_bf[:, :],
                        in_=w_outT.rearrange("(c p) f -> p c f", p=128)
                        ).then_inc(s_yd, 16)
            for r in range(16):
                if r >= 2:
                    g.wait_ge(s_y1, r - 1)
                g.dma_start(out=yrow[:, (r % 2) * S:(r % 2 + 1) * S],
                            in_=ysT[128 * r:128 * (r + 1), :]
                            ).then_inc(ds_y.sem(r), 16)

        @blk.sync
        def _(sy):
            sy = P.chain(sy)
            for r in range(16):
                if r >= 2:
                    sy.wait_ge(s_y1, r - 1)
                sy.dma_start(out=srow[:, (r % 2) * S:(r % 2 + 1) * S],
                             in_=szT[128 * r:128 * (r + 1), :]
                             ).then_inc(ds_s.sem(r), 16)

        @blk.vector
        def _(v):
            v = P.chain(v)
            for r in range(16):
                ds_y.wait(v, r)
                ds_s.wait(v, r)
                v.tensor_tensor(y1bf[:, r * S:(r + 1) * S],
                                yrow[:, (r % 2) * S:(r % 2 + 1) * S],
                                srow[:, (r % 2) * S:(r % 2 + 1) * S],
                                ALU.mult).then_inc(s_y1, 1)
            v.wait_ge(s_yd, 16)

    if phases == "y1":
        with nc.Block() as blk:
            @blk.sync
            def _(sy):
                sy = P.chain(sy)
                sy.wait_ge(s_y1, 16)
                sy.dma_start(out=x2_o[:, :], in_=x1[:, :]).then_inc(s_ld, 16)
                sy.dma_start(out=m3_o[0:1, 0:1], in_=mean[0:1, 0:1]
                             ).then_inc(s_ld, 16)
                sy.wait_ge(s_ld, 80)
        while P._regions:
            P.pop_region()
        return P.close()

    with nc.Block() as blk:
        @blk.gpsimd
        def _(g):
            g = P.chain(g)
            for tt in range(NT):
                if tt >= 2:
                    g.wait_ge(s_hv, 2 * (tt - 1))
                g.dma_start(out=p2t[:, (tt % 2) * D:(tt % 2 + 1) * D],
                            in_=part2[128 * tt:128 * (tt + 1), :]
                            ).then_inc(ds_p2.sem(tt), 16)
                g.dma_start(out=x1t[:, (tt % 2) * D:(tt % 2 + 1) * D],
                            in_=x1[128 * tt:128 * (tt + 1), :]
                            ).then_inc(ds_p2.sem(tt), 16)

        @blk.tensor
        def _(t):
            t = P.chain(t)
            for tt in range(NT):
                for fi, (f0, fw) in enumerate(chunks_of(D, 512)):
                    if tt >= 1:
                        t.wait_ge(s_hv, 2 * (tt - 1) + 1)
                    for di in range(16):
                        inst = t.matmul(
                            ps_mo[fi][:, 0:fw],
                            y1bf[:, di * S + 128 * tt: di * S + 128 * tt + 128],
                            wo_bf[:, di * D + f0: di * D + f0 + fw],
                            start=(di == 0), stop=(di == 15))
                    inst.then_inc(s_mm, 1)

        @blk.vector
        def _(v):
            v = P.chain(v)
            for tt in range(NT):
                pb = (tt % 2) * D
                ds_p2.wait(v, tt)
                for fi, (f0, fw) in enumerate(chunks_of(D, 512)):
                    v.wait_ge(s_mm, 2 * tt + fi + 1)
                    v.tensor_tensor(hres[:, f0:f0 + fw], ps_mo[fi][:, 0:fw],
                                    p2t[:, pb + f0:pb + f0 + fw], ALU.add)
                v.tensor_scalar(scr1[:, :], x1t[:, pb:pb + D],
                                mean[:, tt:tt + 1], rstd[:, tt:tt + 1],
                                ALU.subtract, ALU.mult)
                v.tensor_add(hres[:, :], hres[:, :], scr1[:, :])
                v.tensor_mul(scr1[:, :], hres[:, :], hres[:, :])
                v.reduce_sum(rms_t[:, (tt % 2) * 3:(tt % 2) * 3 + 1],
                             scr1[:, :], axis=AX).then_inc(s_hv, 1)
                v.wait_ge(s_rs, tt + 1)
                if tt >= 2:
                    ds_xb.wait(v, tt - 2)
                v.tensor_scalar(x1b_sb[:, pb:pb + D], hres[:, :],
                                rms_t[:, (tt % 2) * 3 + 2:(tt % 2) * 3 + 3],
                                None, ALU.mult)
                v.tensor_tensor(x1b_sb[:, pb:pb + D], x1b_sb[:, pb:pb + D],
                                n1b[:, :], ALU.mult).then_inc(s_hv, 1)

        @blk.scalar
        def _(s):
            s = P.chain(s)
            for tt in range(NT + 1):
                if tt >= 1:
                    q = ((tt - 1) % 2) * 3
                    s.activation(rms_t[:, q + 2:q + 3], rms_t[:, q + 1:q + 2],
                                 AF.Exp, scale=-0.5).then_inc(s_rs, 1)
                if tt < NT:
                    s.wait_ge(s_hv, 2 * tt + 1)
                    q = (tt % 2) * 3
                    s.activation(rms_t[:, q + 1:q + 2], rms_t[:, q:q + 1],
                                 AF.Ln, bias=eps_r[:, 0:1], scale=1.0 / D)

        @blk.sync
        def _(sy):
            sy = P.chain(sy)
            for tt in range(NT):
                sy.wait_ge(s_hv, 2 * tt + 2)
                pb = (tt % 2) * D
                sy.dma_start(out=x1b_scr[128 * tt:128 * (tt + 1), :],
                             in_=x1b_sb[:, pb:pb + D]).then_inc(ds_xb.sem(tt), 16)
            ds_xb.wait_all(sy, NT)
    P.pop_region()

    if "b" not in phases:
        with nc.Block() as blk:
            @blk.sync
            def _(sy):
                sy = P.chain(sy)
                sy.dma_start(out=x2_o[:, :], in_=x1[:, :]).then_inc(s_ld, 16)
                sy.dma_start(out=m3_o[0:1, 0:1], in_=mean[0:1, 0:1]
                             ).then_inc(s_ld, 16)
                sy.wait_ge(s_ld, 80)
        while P._regions:
            P.pop_region()
        return P.close()

    w1qT = P.sb([128, 8 * DF], BF16, "w1qT")
    w2qT_scr = P.nc.dram_tensor("w2qT_scr", [128, 32 * D], BF16).ap()
    gscb = P.sb([128, 4], F32, "gscb")

    P.push_region()
    wtile = P.sb([128, 2 * DF], F32, "wtile")
    wtmp = P.sb([128, 4], F32)
    wsum1 = P.sb([128, 1], F32)
    wsum2 = P.sb([128, 1], F32)
    tot = P.sb([1, 8], F32)
    ri = P.sb([128, 2048], I32)
    rf = P.sb([128, 2048], F32)
    q_bf = P.sb([128, DF], BF16)
    w2st = P.sb([128, 2 * 2048], BF16, "w2st")
    ps_tpb = P.ps([128, 2048], BF16, "ps_tpb")
    s_w2s = P.sem("s_w2s")
    ds_w2d = P.dsem(2, 1, "dsw2d")

    ds_w1 = P.dsem(2, 1, "dsw1")
    ds_w2l = P.dsem(2, 1, "dsw2l")
    s_ws = P.sem("s_ws")
    s_q = P.sem("s_q")
    s_tp = P.sem("s_tp")
    s_ev = P.sem("s_ev")

    WS = ((w1, 32, D, 0), (w2, 8, DF, 1))

    with nc.Block() as blk:
        @blk.gpsimd
        def _(g):
            g = P.chain(g)
            k = 0
            for src, nr, wd, wi in WS:
                for r in range(nr):
                    if k >= 2:
                        g.wait_ge(s_ws, k - 1)
                    g.dma_start(out=wtile[:, (k % 2) * DF:(k % 2) * DF + wd],
                                in_=src[128 * r:128 * (r + 1), :]
                                ).then_inc(ds_w1.sem(k), 16)
                    k += 1

        @blk.vector
        def _(v):
            v = P.chain(v)
            v.memset(wsum1[:, :], 0.0)
            v.memset(wsum2[:, :], 0.0)
            k = 0
            for src, nr, wd, wi in WS:
                for r in range(nr):
                    ds_w1.wait(v, k)
                    v.tensor_reduce(wtmp[:, k % 2:k % 2 + 1],
                                    wtile[:, (k % 2) * DF:(k % 2) * DF + wd],
                                    axis=AX, op=ALU.add,
                                    apply_absolute_value=True).then_inc(s_ws, 1)
                    if k >= 1:
                        dst = wsum1 if (k - 1) < 32 else wsum2
                        v.tensor_add(dst[:, 0:1], dst[:, 0:1],
                                     wtmp[:, (k - 1) % 2:(k - 1) % 2 + 1])
                    k += 1
            v.tensor_add(wsum2[:, 0:1], wsum2[:, 0:1],
                         wtmp[:, (k - 1) % 2:(k - 1) % 2 + 1])

    emit_partition_reduce(P, wsum1, tot[0:1, 0:1], ALU.add)
    emit_partition_reduce(P, wsum2, tot[0:1, 1:2], ALU.add)

    with nc.Block() as blk:
        @blk.vector
        def _(v):
            v = P.chain(v)
            v.tensor_scalar(tot[0:1, 4:5], tot[0:1, 0:1], 1.0 / (DF * D),
                            1e-5, ALU.mult, ALU.max)
            v.tensor_scalar(tot[0:1, 6:7], tot[0:1, 1:2], 1.0 / (DF * D),
                            1e-5, ALU.mult, ALU.max)
            v.reciprocal(tot[0:1, 5:6], tot[0:1, 4:5])
            v.reciprocal(tot[0:1, 7:8], tot[0:1, 6:7])
    emit_bounce_bcast(P, tot[0:1, 4:8], 4, gscb)

    units = []
    tiles = []
    k = 0
    for src_, nr, wd, wi in WS:
        nb = wd // 128
        for r in range(nr):
            tiles.append((src_, nr, wd, wi, r, k))
            for gsel in range(0, nb, 16):
                units.append((k, wi, r, gsel, min(16, nb - gsel)))
            k += 1
    n_tiles = len(tiles)
    tile_units_end = {}
    cnt = 0
    for (kk, wi, r, gsel, gn) in units:
        cnt += 1
        tile_units_end[kk] = cnt

    with nc.Block() as blk:
        @blk.gpsimd
        def _(g):
            g = P.chain(g)
            for src_, nr, wd, wi, r, kk in tiles:
                if kk >= 2:
                    g.wait_ge(s_q, kk - 1)
                g.dma_start(out=wtile[:, (kk % 2) * DF:(kk % 2) * DF + wd],
                            in_=src_[128 * r:128 * (r + 1), :]
                            ).then_inc(ds_w2l.sem(kk), 16)

        @blk.vector
        def _(v):
            v = P.chain(v)
            for src_, nr, wd, wi, r, kk in tiles:
                ds_w2l.wait(v, kk)
                if kk >= 1:
                    v.wait_ge(s_tp, tile_units_end[kk - 1])
                inst = None
                for c0 in range(0, wd, 2048):
                    Hc = min(2048, wd - c0)
                    v.tensor_scalar(ri[:, 0:Hc],
                                    wtile[:, (kk % 2) * DF + c0:
                                          (kk % 2) * DF + c0 + Hc],
                                    gscb[:, 1 + 2 * wi:2 + 2 * wi], None,
                                    ALU.mult)
                    v.tensor_scalar(rf[:, 0:Hc], ri[:, 0:Hc],
                                    1.0, -1.0, ALU.min, ALU.max)
                    inst = v.tensor_scalar(q_bf[:, c0:c0 + Hc], rf[:, 0:Hc],
                                           gscb[:, 2 * wi:1 + 2 * wi], None,
                                           ALU.mult)
                inst.then_inc(s_q, 1)

        @blk.tensor
        def _(t):
            t = P.chain(t)
            for ui, (kk, wi, r, gsel, gn) in enumerate(units):
                t.wait_ge(s_q, kk + 1)
                if ui >= 1:
                    t.wait_ge(s_ev, ui)
                for j in range(gn):
                    inst = t.transpose(ps_tpb[:, 128 * j:128 * (j + 1)],
                                       q_bf[:, 128 * (gsel + j):
                                            128 * (gsel + j + 1)],
                                       P.ident[:, :])
                inst.then_inc(s_tp, 1)

        @blk.scalar
        def _(s):
            s = P.chain(s)
            nw2 = 0
            for ui, (kk, wi, r, gsel, gn) in enumerate(units):
                s.wait_ge(s_tp, ui + 1)
                srcap = sb_ap3(ps_tpb, 0, 128, gn, 128)
                if wi == 0:
                    dst = sb_ap3(w1qT, 128 * r, DF, gn, 128)
                    s.copy(dst, srcap).then_inc(s_ev, 1)
                else:
                    if nw2 >= 2:
                        ds_w2d.wait(s, nw2 - 2)
                    dst = bass.AP(w2st, (nw2 % 2) * 2048,
                                  [[w2st.shape[1], 128], [128, gn], [1, 128]])
                    s.copy(dst, srcap).then_inc(s_ev, 1)
                    s.copy(w2st[:, 0:1] if False else gscb[:, 0:1],
                           gscb[:, 0:1]).then_inc(s_w2s, 1)
                    nw2 += 1

        @blk.sync
        def _(sy):
            sy = P.chain(sy)
            nw2 = 0
            for ui, (kk, wi, r, gsel, gn) in enumerate(units):
                if wi != 0:
                    sy.wait_ge(s_w2s, nw2 + 1)
                    dst = bass.AP(w2qT_scr.tensor, gsel * D + 128 * r,
                                  [[32 * D, 128], [D, gn], [1, 128]])
                    sy.dma_start(out=dst,
                                 in_=bass.AP(w2st, (nw2 % 2) * 2048,
                                             [[w2st.shape[1], 128], [128, gn],
                                              [1, 128]])).then_inc(ds_w2d.sem(nw2), 16)
                    nw2 += 1
            ds_w2d.wait_all(sy, nw2)
    P.pop_region()
    if "c" not in phases:
        with nc.Block() as blk:
            @blk.sync
            def _(sy):
                sy = P.chain(sy)
                sy.dma_start(out=x2_o[:, :], in_=x1[:, :]).then_inc(s_ld, 16)
                sy.dma_start(out=m3_o[0:1, 0:1], in_=mean[0:1, 0:1]
                             ).then_inc(s_ld, 16)
                sy.wait_ge(s_ld, 80)
        while P._regions:
            P.pop_region()
        return P.close()

    fqT_scr = P.dscr("fqT_scr", [S, DF])
    fqT_scr = P.nc.dram_tensor("fqT_scrb", [S, DF], BF16).ap()
    si2_scr = P.dscr("si2_scr", [128, NT])

    P.push_region()
    b1bc = P.sb([128, DF], F32, "b1bc")
    xb_t = P.sb([128, 2 * D], F32, "xb_t")
    qs = P.sb([128, 8], F32)
    ri_c = P.sb([128, 2048], I32)
    rf_c = P.sb([128, 2048], F32)
    xq_bf = P.sb([128, D], BF16)
    xqT_t = P.sb([128, 2 * 8 * 128], BF16)
    f_sb = P.sb([128, DF], F32, "f_sb")
    t1_c = P.sb([128, 2048], F32)
    t2_c = P.sb([128, 2048], F32)
    q2bf = P.sb([128, DF], BF16)
    fqT_t = P.sb([128, 32 * 128], BF16)
    si2_all = P.sb([128, NT], F32, "si2_all")
    ps_tpc = P.ps([128, 2048], BF16, "ps_tpc")
    ps_m1 = [P.ps([128, 512], F32, f"psm1{i}") for i in range(2)]

    s_b1 = P.sem("s_b1")
    ds_xb2 = P.dsem(2, 1, "dsxb2")
    s_cq = P.sem("s_cq")
    s_ct = P.sem("s_ct")
    s_ce = P.sem("s_ce")
    s_cm = P.sem("s_cm")
    s_cf = P.sem("s_cf")
    s_cg = P.sem("s_cg")
    s_ft = P.sem("s_ft")
    s_fe = P.sem("s_fe")
    ds_fq = P.dsem(2, 1, "dsfq")
    s_si = P.sem("s_si")

    stage = 4
    if phases.startswith("c1s"):
        stage = int(phases[3])

    with nc.Block() as blk:
        @blk.sync
        def _(sy):
            sy = P.chain(sy)
            sy.dma_start(out=b1bc[:, :],
                         in_=bass.AP(b1.tensor, 0, [[0, 128], [1, DF]])
                         ).then_inc(s_b1, 16)
            for tt in range(NT):
                if tt >= 2:
                    sy.wait_ge(s_cq, tt - 1)
                sy.dma_start(out=xb_t[:, (tt % 2) * D:(tt % 2 + 1) * D],
                             in_=x1b_scr[128 * tt:128 * (tt + 1), :]
                             ).then_inc(ds_xb2.sem(tt), 16)

        @blk.vector
        def _(v):
            v = P.chain(v)
            for tt in range(NT):
                if tt == 0:
                    v.wait_ge(s_b1, 16)
                ds_xb2.wait(v, tt)
                pb = (tt % 2) * D
                xb = xb_t[:, pb:pb + D]
                v.tensor_reduce(qs[:, 0:1], xb, axis=AX, op=ALU.max,
                                apply_absolute_value=True)
                v.tensor_scalar(qs[:, 1:2], qs[:, 0:1], 1e-5, 1.0 / 127.0,
                                ALU.max, ALU.mult)
                v.reciprocal(qs[:, 2:3], qs[:, 1:2])
                if tt >= 1:
                    v.wait_ge(s_ct, tt)
                v.tensor_scalar(ri_c[:, 0:D], xb_t[:, pb:pb + D],
                                qs[:, 2:3], None, ALU.mult)
                v.tensor_copy(xq_bf[:, :], ri_c[:, 0:D]).then_inc(s_cq, 1)
                if stage < 3:
                    continue
                v.wait_ge(s_cf, 10 * tt + 8)
                f = f_sb
                for c0 in range(0, DF, 2048):
                    Hc = 1024
                    for hb in (0, Hc):
                        v.tensor_tensor(f[:, c0 + hb:c0 + hb + Hc],
                                        f[:, c0 + hb:c0 + hb + Hc],
                                        b1bc[:, c0 + hb:c0 + hb + Hc], ALU.add)
                for ci2, c0 in enumerate(range(0, DF, 2048)):
                    Hc = 1024
                    for hb in (0, Hc):
                        v.tensor_mul(t1_c[:, hb:hb + Hc],
                                     f[:, c0 + hb:c0 + hb + Hc],
                                     f[:, c0 + hb:c0 + hb + Hc])
                    for hb in (0, Hc):
                        v.tensor_scalar(t1_c[:, hb:hb + Hc], t1_c[:, hb:hb + Hc],
                                        GC1 * GC2, GC1, ALU.mult, ALU.add)
                    inst = None
                    for hb in (0, Hc):
                        inst = v.tensor_mul(t2_c[:, hb:hb + Hc],
                                            t1_c[:, hb:hb + Hc],
                                            f[:, c0 + hb:c0 + hb + Hc])
                    inst.then_inc(s_cg, 1)
                    v.wait_ge(s_cf, 10 * tt + 8 + ci2 + 1)
                    for hb in (0, Hc):
                        v.tensor_scalar(t1_c[:, hb:hb + Hc], t1_c[:, hb:hb + Hc],
                                        1.0, 0.5, ALU.add, ALU.mult)
                    for hb in (0, Hc):
                        v.tensor_mul(f[:, c0 + hb:c0 + hb + Hc],
                                     t1_c[:, hb:hb + Hc],
                                     f[:, c0 + hb:c0 + hb + Hc])
                if stage < 4:
                    v.tensor_copy(t1_c[:, 0:1], t1_c[:, 0:1]).then_inc(s_cg, 1)
                    continue
                v.tensor_reduce(qs[:, 0:1], f[:, :], axis=AX, op=ALU.max,
                                apply_absolute_value=True)
                v.tensor_scalar(qs[:, 1:2], qs[:, 0:1], 1e-5, 1.0 / 127.0,
                                ALU.max, ALU.mult)
                v.reciprocal(qs[:, 2:3], qs[:, 1:2])
                v.tensor_copy(si2_all[:, tt:tt + 1], qs[:, 1:2])
                if tt >= 1:
                    v.wait_ge(s_ft, 2 * tt)
                inst = None
                for c0 in range(0, DF, 2048):
                    v.tensor_scalar(ri_c[:, 0:2048], f[:, c0:c0 + 2048],
                                    qs[:, 2:3], None, ALU.mult)
                    inst = v.tensor_copy(q2bf[:, c0:c0 + 2048], ri_c[:, 0:2048])
                inst.then_inc(s_cg, 1)

        @blk.tensor
        def _(t):
            t = P.chain(t)
            for tt in range(NT):
                t.wait_ge(s_cq, tt + 1)
                if tt >= 1:
                    t.wait_ge(s_ce, 3 * tt)
                    t.wait_ge(s_fe, 2 * tt)
                for j in range(8):
                    inst = t.transpose(ps_tpc[:, 128 * j:128 * (j + 1)],
                                       xq_bf[:, 128 * j:128 * (j + 1)],
                                       P.ident[:, :])
                inst.then_inc(s_ct, 1)
                if stage < 2:
                    continue
                t.wait_ge(s_ce, 3 * tt + 1)
                qb = (tt % 2) * 1024
                for fi in range(8):
                    if fi >= 2:
                        t.wait_ge(s_cf, 10 * tt + fi - 1)
                    elif tt >= 1:
                        t.wait_ge(s_cf, 10 * tt - 2)
                    for ci in range(8):
                        inst = t.matmul(
                            ps_m1[fi % 2][:, :],
                            xqT_t[:, qb + 128 * ci:qb + 128 * ci + 128],
                            w1qT[:, ci * DF + 512 * fi: ci * DF + 512 * (fi + 1)],
                            start=(ci == 0), stop=(ci == 7))
                    inst.then_inc(s_cm, 1)
                if stage < 4:
                    continue
                t.wait_ge(s_cg, 3 * tt + 3)
                for gi in range(2):
                    if 2 * tt + gi >= 1:
                        t.wait_ge(s_fe, 2 * tt + gi)
                    for j in range(16):
                        inst = t.transpose(ps_tpc[:, 128 * j:128 * (j + 1)],
                                           q2bf[:, 128 * (16 * gi + j):
                                                128 * (16 * gi + j + 1)],
                                           P.ident[:, :])
                    inst.then_inc(s_ft, 1)

        @blk.scalar
        def _(s):
            s = P.chain(s)
            for tt in range(NT):
                s.wait_ge(s_ct, tt + 1)
                qb = (tt % 2) * 1024
                s.copy(bass.AP(xqT_t, qb, [[xqT_t.shape[1], 128], [128, 8],
                                           [1, 128]]),
                       sb_ap3(ps_tpc, 0, 128, 8, 128)).then_inc(s_ce, 1)
                if stage < 2:
                    s.copy(qs[:, 7:8], gscb[:, 0:1]).then_inc(s_ce, 2)
                    continue
                if tt >= 1 and stage >= 3:
                    s.wait_ge(s_cg, 3 * tt)
                for fi in range(8):
                    s.wait_ge(s_cm, 8 * tt + fi + 1)
                    s.activation(f_sb[:, 512 * fi:512 * (fi + 1)],
                                 ps_m1[fi % 2][:, :], AF.Identity,
                                 scale=qs[:, 1:2]).then_inc(s_cf, 1)
                if stage < 3:
                    s.copy(qs[:, 7:8], gscb[:, 0:1]).then_inc(s_ce, 2)
                    continue
                for ci2 in range(2):
                    s.wait_ge(s_cg, 3 * tt + ci2 + 1)
                    Hc = 1024
                    inst = None
                    for hb in (0, Hc):
                        inst = s.activation(t1_c[:, hb:hb + Hc],
                                            t2_c[:, hb:hb + Hc], AF.Tanh)
                    inst.then_inc(s_cf, 1)
                if stage < 4:
                    s.copy(qs[:, 7:8], gscb[:, 0:1]).then_inc(s_ce, 2)
                    continue
                if tt >= 1:
                    ds_fq.wait(s, tt - 1)
                for gi in range(2):
                    s.wait_ge(s_ft, 2 * tt + gi + 1)
                    dst = bass.AP(fqT_t, 128 * 16 * gi,
                                  [[fqT_t.shape[1], 128], [128, 16], [1, 128]])
                    s.copy(dst, sb_ap3(ps_tpc, 0, 128, 16, 128)
                           ).then_inc(s_fe, 1).then_inc(s_ce, 1) if False else \
                        s.copy(dst, sb_ap3(ps_tpc, 0, 128, 16, 128)
                               ).then_inc(s_fe, 1)
                s.copy(qs[:, 7:8], gscb[:, 0:1]).then_inc(s_ce, 2)

        @blk.gpsimd
        def _(g):
            g = P.chain(g)
            if stage >= 4:
                for tt in range(NT):
                    g.wait_ge(s_fe, 2 * (tt + 1))
                    g.dma_start(out=fqT_scr[128 * tt:128 * (tt + 1), :],
                                in_=fqT_t[:, :]).then_inc(ds_fq.sem(tt), 16)
                g.wait_ge(s_cg, 3 * NT)
                g.dma_start(out=si2_scr[:, :], in_=si2_all[:, :]).then_inc(s_si, 16)
                ds_fq.wait_all(g, NT)
                g.wait_ge(s_si, 16)
    P.pop_region()

    if "2" in phases and "c2skip" == phases:
        pass
    if phases == "abc1":
        with nc.Block() as blk:
            @blk.sync
            def _(sy):
                sy = P.chain(sy)
                sy.dma_start(out=x2_o[:, :], in_=x1[:, :]).then_inc(s_ld, 16)
                sy.dma_start(out=m3_o[0:1, 0:1], in_=n1b[0:1, 0:1]
                             ).then_inc(s_ld, 16)
                sy.wait_ge(s_ld, 80)
        while P._regions:
            P.pop_region()
        return P.close()

    if stage != 4 or phases == "abc1x":
        with nc.Block() as blk:
            @blk.sync
            def _(sy):
                sy = P.chain(sy)
                sy.dma_start(out=x2_o[:, :], in_=x1[:, :]).then_inc(s_ld, 16)
                sy.dma_start(out=m3_o[0:1, 0:1], in_=n1b[0:1, 0:1]
                             ).then_inc(s_ld, 16)
                sy.wait_ge(s_ld, 80)
        while P._regions:
            P.pop_region()
        return P.close()

    P.push_region()
    w2qT = P.sb([128, 32 * D], BF16, "w2qT")
    fq_st = P.sb([128, 2 * 32 * 128], BF16, "fq_st")
    si2_t = P.sb([128, NT], F32)
    xb2_t = P.sb([128, 2 * D], F32)
    x1_t2 = P.sb([128, 2 * D], F32)
    f2_sb = P.sb([128, D], F32)
    sc2 = P.sb([128, 8], F32)
    x2_sb = P.sb([128, 2 * D], F32)
    xlt = P.sb([128, 2 * D], F32)
    macc = P.sb([128, 4], F32)
    m3s = P.sb([1, 1], F32)
    ps_m2 = [P.ps([128, 512], F32, f"psm2{i}") for i in range(2)]

    s_st2 = P.sem("s_st2")
    ds_c2 = P.dsem(2, 3, "dsc2")
    s_m2 = P.sem("s_m2")
    s_v2 = P.sem("s_v2")
    s_a2 = P.sem("s_a2")
    ds_o2 = P.dsem(2, 1, "dso2")

    with nc.Block() as blk:
        @blk.gpsimd
        def _(g):
            g = P.chain(g)
            g.memset(macc[:, :], 0.0)

    with nc.Block() as blk:
        @blk.sync
        def _(sy):
            sy = P.chain(sy)
            sy.dma_start(out=w2qT[:, :], in_=w2qT_scr[:, :]).then_inc(s_st2, 16)
            sy.dma_start(out=si2_t[:, :], in_=si2_scr[:, :]).then_inc(s_st2, 16)
            for tt in range(NT):
                if tt >= 2:
                    sy.wait_ge(s_v2, 4 * (tt - 1))
                pb = (tt % 2)
                sy.dma_start(out=fq_st[:, pb * 4096:(pb + 1) * 4096],
                             in_=fqT_scr[128 * tt:128 * (tt + 1), :]
                             ).then_inc(ds_c2.sem(tt), 16)
                sy.dma_start(out=xb2_t[:, pb * D:(pb + 1) * D],
                             in_=x1b_scr[128 * tt:128 * (tt + 1), :]
                             ).then_inc(ds_c2.sem(tt), 16)
                sy.dma_start(out=x1_t2[:, pb * D:(pb + 1) * D],
                             in_=x1[128 * tt:128 * (tt + 1), :]
                             ).then_inc(ds_c2.sem(tt), 16)

        @blk.tensor
        def _(t):
            t = P.chain(t)
            for tt in range(NT):
                if tt == 0:
                    t.wait_ge(s_st2, 32)
                ds_c2.wait(t, tt)
                pb = (tt % 2)
                for fi in range(2):
                    if tt >= 1:
                        t.wait_ge(s_v2, 4 * (tt - 1) + fi + 1)
                    for ci in range(32):
                        inst = t.matmul(
                            ps_m2[fi][:, :],
                            fq_st[:, pb * 4096 + 128 * ci:
                                  pb * 4096 + 128 * ci + 128],
                            w2qT[:, ci * D + 512 * fi: ci * D + 512 * (fi + 1)],
                            start=(ci == 0), stop=(ci == 31))
                    inst.then_inc(s_m2, 1)

        @blk.vector
        def _(v):
            v = P.chain(v)
            for tt in range(NT):
                pb = (tt % 2)
                v.wait_ge(s_a2, 4 * tt + 2)
                v.tensor_tensor(f2_sb[:, :], f2_sb[:, :], b2b[:, :], ALU.add)
                v.tensor_tensor(f2_sb[:, :], f2_sb[:, :],
                                xb2_t[:, pb * D:(pb + 1) * D], ALU.add)
                v.tensor_mul(xlt[:, pb * D:(pb + 1) * D], f2_sb[:, :],
                             f2_sb[:, :])
                v.reduce_sum(sc2[:, (tt % 2) * 3:(tt % 2) * 3 + 1],
                             xlt[:, pb * D:(pb + 1) * D],
                             axis=AX).then_inc(s_v2, 1)
                v.wait_ge(s_a2, 4 * tt + 3)
                v.tensor_scalar(f2_sb[:, :], f2_sb[:, :],
                                sc2[:, (tt % 2) * 3 + 2:(tt % 2) * 3 + 3],
                                None, ALU.mult)
                v.tensor_tensor(f2_sb[:, :], f2_sb[:, :], n2b[:, :], ALU.mult)
                if tt >= 2:
                    ds_o2.wait(v, tt - 2)
                v.tensor_add(x2_sb[:, pb * D:(pb + 1) * D], f2_sb[:, :],
                             x1_t2[:, pb * D:(pb + 1) * D]).then_inc(s_v2, 1)
                xa = x2_sb[:, pb * D:(pb + 1) * D]
                v.reduce_sum(sc2[:, 3:4], xa, axis=AX)
                v.tensor_mul(xlt[:, pb * D:(pb + 1) * D], xa, xa)
                v.reduce_sum(sc2[:, 4:5], xlt[:, pb * D:(pb + 1) * D], axis=AX)
                v.tensor_scalar_mul(sc2[:, 3:4], sc2[:, 3:4], 1.0 / D)
                v.tensor_mul(sc2[:, 5:6], sc2[:, 3:4], sc2[:, 3:4])
                v.scalar_tensor_tensor(sc2[:, 4:5], sc2[:, 4:5], 1.0 / D,
                                       sc2[:, 5:6], ALU.mult,
                                       ALU.subtract).then_inc(s_v2, 1)
                v.wait_ge(s_a2, 4 * tt + 4)
                v.tensor_scalar(xlt[:, pb * D:(pb + 1) * D], xa, sc2[:, 3:4],
                                sc2[:, 6:7], ALU.subtract, ALU.mult)
                v.tensor_reduce(sc2[:, 5:6], xlt[:, pb * D:(pb + 1) * D],
                                axis=AX, op=ALU.max, apply_absolute_value=True)
                v.tensor_max(macc[:, 0:1], macc[:, 0:1],
                             sc2[:, 5:6]).then_inc(s_v2, 1)

        @blk.scalar
        def _(s):
            s = P.chain(s)
            for tt in range(NT):
                pb = (tt % 2)
                if tt >= 1:
                    s.wait_ge(s_v2, 4 * (tt - 1) + 2)
                for fi in range(2):
                    s.wait_ge(s_m2, 2 * tt + fi + 1)
                    s.activation(f2_sb[:, 512 * fi:512 * (fi + 1)],
                                 ps_m2[fi][:, :], AF.Identity,
                                 scale=si2_t[:, tt:tt + 1]).then_inc(s_a2, 1)
                s.wait_ge(s_v2, 4 * tt + 1)
                s.activation(sc2[:, (tt % 2) * 3 + 1:(tt % 2) * 3 + 2],
                             sc2[:, (tt % 2) * 3:(tt % 2) * 3 + 1], AF.Ln,
                             bias=eps_r[:, 0:1], scale=1.0 / D)
                s.activation(sc2[:, (tt % 2) * 3 + 2:(tt % 2) * 3 + 3],
                             sc2[:, (tt % 2) * 3 + 1:(tt % 2) * 3 + 2], AF.Exp,
                             scale=-0.5).then_inc(s_a2, 1)
                s.wait_ge(s_v2, 4 * tt + 3)
                s.activation(sc2[:, 4:5], sc2[:, 4:5], AF.Ln,
                             bias=eps_l[:, 0:1])
                s.activation(sc2[:, 6:7], sc2[:, 4:5], AF.Exp,
                             scale=-0.5).then_inc(s_a2, 1)

        @blk.gpsimd
        def _(g):
            g = P.chain(g)
            for tt in range(NT):
                g.wait_ge(s_v2, 4 * tt + 2)
                pb = (tt % 2)
                g.dma_start(out=x2_o[128 * tt:128 * (tt + 1), :],
                            in_=x2_sb[:, pb * D:(pb + 1) * D]
                            ).then_inc(ds_o2.sem(tt), 16)
            ds_o2.wait_all(g, NT)
    emit_partition_max(P, macc, m3s)
    with nc.Block() as blk:
        @blk.sync
        def _(sy):
            sy = P.chain(sy)
            sy.dma_start(out=m3_o[0:1, 0:1], in_=m3s[0:1, 0:1]).then_inc(s_ld, 16)
            sy.wait_ge(s_ld, 64)
    P.pop_region()
    return P.close()



F_CH = chunks_of(D, 512)


def build_L8():
    P = Prog()
    nc = P.nc
    xs = P.din("xs", [S, D])
    h_in = P.din("h", [S, DI])
    M8 = P.din("m8", [1, 8])
    clip_d = P.din("clip_d", [1, 1])
    wd = P.din("wd", [D, DI])
    wf = P.din("wf", [D, D])
    a1 = P.din("a1", [D, D])
    a2 = P.din("a2", [D, D])
    fb = P.din("fb", [1, D])
    WqT = P.din("WqT", [D, D])
    bq = P.din("bq", [1, D])
    WkvT = P.din("WkvT", [D, 128])
    bkv = P.din("bkv", [1, 128])

    x4_o = P.dout("x4", [S, D])
    q_o = P.dout("qbf", [S, D], BF16)
    kv_o = P.dout("kvbf", [S, 128], BF16)

    emit_ident(P)
    eps_l = P.constf(1e-5)

    x3_sb = P.sb([128, NT * D], F32, "x3_sb")
    mean = P.sb([128, NT], F32)
    rstd = P.sb([128, NT], F32)
    wf_sb = P.sb([128, 8 * D], BF16, "wf_sb")
    a1_sb = P.sb([128, 8 * D], BF16, "a1_sb")
    wkv_sb = P.sb([128, 8 * 128], BF16, "wkv_sb")
    s_w1 = P.sem("s_w1")

    sc = P.sb([1, 8], F32)
    clp = P.sb([1, 1], F32)
    scb = P.sb([128, 3], F32)
    sd = P.sem("sd")
    with nc.Block() as blk:
        @blk.sync
        def _(sy):
            sy = P.chain(sy)
            sy.dma_start(out=x3_sb[:, :],
                         in_=xs.rearrange("(t p) d -> p t d", p=128)
                         ).then_inc(sd, 16)
            sy.wait_ge(sd, 16)
    emit_gscale(P, M8, clip_d, sc, clp, scb)

    P.push_region()
    NCI = DI // 128
    htile = P.sb([128, 2 * DI], F32)
    ri = P.sb([128, D], I32)
    rf = P.sb([128, D], F32)
    qh_bf = P.sb([128, DI], BF16)
    hqT = P.sb([128, 2 * NCI * 128], BF16)
    wrow = P.sb([128, 8], F32)
    wdT = P.sb([128, NCI * D], BF16, "wdT")
    ps_tph = P.ps([128, NCI * 128], BF16, "ps_tph")
    ps_mm = [P.ps([128, 512], F32, f"psmm{i}") for i in range(2)]

    ds_t = P.dsem(2, 1, "dst8")
    s_q = P.sem("s_q")
    s_tp = P.sem("s_tp")
    s_ev = P.sem("s_ev")
    s_mm = P.sem("s_mm")
    s_xv = P.sem("s_xv")

    NRW = D // 128
    with nc.Block() as blk:
        @blk.gpsimd
        def _(g):
            g = P.chain(g)
            for r in range(NRW):
                if r >= 2:
                    g.wait_ge(s_q, r - 1)
                g.dma_start(out=htile[:, (r % 2) * DI:(r % 2 + 1) * DI],
                            in_=wd[128 * r:128 * (r + 1), :]
                            ).then_inc(ds_t.sem(r), 16)

        @blk.vector
        def _(v):
            v = P.chain(v)
            for r in range(NRW):
                ds_t.wait(v, r)
                if r >= 1:
                    v.wait_ge(s_tp, r)
                wt = htile[:, (r % 2) * DI:(r % 2 + 1) * DI]
                v.tensor_reduce(wrow[:, 0:1], wt, axis=AX, op=ALU.max,
                                apply_absolute_value=True)
                v.tensor_scalar_mul(wrow[:, 3:4], wrow[:, 0:1], 1.0 / 127.0)
                v.reciprocal(wrow[:, 1:2], wrow[:, 0:1])
                v.tensor_scalar_mul(wrow[:, 2:3], wrow[:, 1:2], 127.0)
                inst = None
                for c0, cw in chunks_of(DI, D):
                    v.tensor_scalar(ri[:, 0:cw],
                                    htile[:, (r % 2) * DI + c0:
                                          (r % 2) * DI + c0 + cw],
                                    wrow[:, 2:3], None, ALU.mult)
                    inst = v.tensor_scalar(qh_bf[:, c0:c0 + cw], ri[:, 0:cw],
                                           wrow[:, 3:4], None, ALU.mult)
                inst.then_inc(s_q, 1)

        @blk.tensor
        def _(t):
            t = P.chain(t)
            for r in range(NRW):
                t.wait_ge(s_q, r + 1)
                if r >= 1:
                    t.wait_ge(s_ev, r)
                for j in range(NCI):
                    inst = t.transpose(ps_tph[:, 128 * j:128 * (j + 1)],
                                       qh_bf[:, 128 * j:128 * (j + 1)],
                                       P.ident[:, :])
                inst.then_inc(s_tp, 1)

        @blk.scalar
        def _(s):
            s = P.chain(s)
            for r in range(NRW):
                s.wait_ge(s_tp, r + 1)
                s.copy(sb_ap3(wdT, 128 * r, D, NCI, 128),
                       sb_ap3(ps_tph, 0, 128, NCI, 128)).then_inc(s_ev, 1)

    with nc.Block() as blk:
        @blk.gpsimd
        def _(g):
            g = P.chain(g)
            for tt in range(NT):
                if tt >= 2:
                    g.wait_ge(s_q, NRW + tt - 1)
                g.dma_start(out=htile[:, (tt % 2) * DI:(tt % 2 + 1) * DI],
                            in_=h_in[128 * tt:128 * (tt + 1), :]
                            ).then_inc(ds_t.sem(NRW + tt), 16)
            g.dma_start(out=wf_sb[:, :],
                        in_=wf.rearrange("(c p) d -> p c d", p=128)
                        ).then_inc(s_w1, 16)
            g.dma_start(out=a1_sb[:, :],
                        in_=a1.rearrange("(c p) d -> p c d", p=128)
                        ).then_inc(s_w1, 16)
            g.dma_start(out=wkv_sb[:, :],
                        in_=WkvT.rearrange("(c p) d -> p c d", p=128)
                        ).then_inc(s_w1, 16)

        @blk.vector
        def _(v):
            v = P.chain(v)

            def qh(k):
                ds_t.wait(v, NRW + k)
                if k >= 1:
                    v.wait_ge(s_tp, NRW + k)
                hb0 = (k % 2) * DI
                inst = None
                for c0, cw in chunks_of(DI, D):
                    v.tensor_scalar(ri[:, 0:cw],
                                    htile[:, hb0 + c0:hb0 + c0 + cw],
                                    scb[:, 1:2], None, ALU.mult)
                    inst = v.tensor_scalar(qh_bf[:, c0:c0 + cw], ri[:, 0:cw],
                                           127.0, -128.0, ALU.min, ALU.max)
                inst.then_inc(s_q, 1)

            qh(0)
            if NT > 1:
                qh(1)
            for tt in range(NT):
                for fi, (f0, fw) in enumerate(F_CH):
                    v.wait_ge(s_mm, 2 * tt + fi + 1)
                    v.scalar_tensor_tensor(
                        x3_sb[:, tt * D + f0:tt * D + f0 + fw],
                        ps_mm[fi][:, 0:fw], scb[:, 2:3],
                        x3_sb[:, tt * D + f0:tt * D + f0 + fw],
                        ALU.mult, ALU.add).then_inc(s_xv, 1)
                if tt + 2 < NT:
                    qh(tt + 2)

        @blk.tensor
        def _(t):
            t = P.chain(t)
            for tt in range(NT):
                t.wait_ge(s_q, NRW + tt + 1)
                if tt >= 2:
                    t.wait_ge(s_ev, NRW + tt - 1)
                qb = (tt % 2) * NCI * 128
                for j in range(NCI):
                    inst = t.transpose(ps_tph[:, 128 * j:128 * (j + 1)],
                                       qh_bf[:, 128 * j:128 * (j + 1)],
                                       P.ident[:, :])
                inst.then_inc(s_tp, 1)
                t.wait_ge(s_ev, NRW + tt + 1)
                for fi, (f0, fw) in enumerate(F_CH):
                    if tt >= 1:
                        t.wait_ge(s_xv, (tt - 1) * 2 + fi + 1)
                    for ci in range(NCI):
                        inst = t.matmul(
                            ps_mm[fi][:, 0:fw],
                            hqT[:, qb + 128 * ci: qb + 128 * ci + 128],
                            wdT[:, ci * D + f0: ci * D + f0 + fw],
                            start=(ci == 0), stop=(ci == NCI - 1))
                    inst.then_inc(s_mm, 1)

        @blk.scalar
        def _(s):
            s = P.chain(s)
            for tt in range(NT):
                s.wait_ge(s_tp, NRW + tt + 1)
                if tt >= 2:
                    s.wait_ge(s_mm, 2 * (tt - 1))
                qb = (tt % 2) * NCI * 128
                s.copy(bass.AP(hqT, qb, [[hqT.shape[1], 128], [128, NCI],
                                         [1, 128]]),
                       sb_ap3(ps_tph, 0, 128, NCI, 128)).then_inc(s_ev, 1)
    P.pop_region()

    emit_ln_stats(P, x3_sb, NT, D, mean, rstd)

    P.push_region()
    a2_sb = P.sb([128, 8 * D], BF16, "a2_sb")
    wq_sb = P.sb([128, 8 * D], BF16, "wq_sb")
    bq8 = P.sb([128, D], F32, "bq8")
    fbb = P.sb([128, D], F32, "fbb")
    bkvb = P.sb([128, 128], F32, "bkvb")
    srcb = P.sb([128, 4 * D], BF16, "srcb")
    xT = P.sb([128, 4 * D], BF16, "xT8")
    x4t = P.sb([128, 2 * D], F32, "x4t")
    h3f = P.sb([128, D], F32, "h3f")
    th = P.sb([128, D], F32, "th8")
    qo = P.sb([128, 2 * D], BF16, "qo")
    kvo = P.sb([128, 2 * 128], BF16, "kvo")
    st = P.sb([128, 4], F32)
    mt = P.sb([128, 1], F32)
    var_t = P.sb([128, 1], F32)
    lv_t = P.sb([128, 1], F32)
    rt_t = P.sb([128, 1], F32)

    ps_tp2 = P.ps([128, 1024], BF16, "ps_tp2")
    ps_q = [P.ps([128, 1024], F32, f"psq{i}") for i in range(3)]

    s_wa2 = P.sem("s_wa2")
    s_wq = P.sem("s_wq")
    s_b = P.sem("s_b")
    s_l1 = P.sem("s_l1")
    s_l2 = P.sem("s_l2")
    s_h1 = P.sem("s_h1")
    s_h2 = P.sem("s_h2")
    s_h3 = P.sem("s_h3")
    s_qe = P.sem("s_qe")
    s_ke = P.sem("s_ke")
    s_t0 = P.sem("s_t0")
    s_t1 = P.sem("s_t1")
    s_t2 = P.sem("s_t2")
    s_t3 = P.sem("s_t3")
    s_x0 = P.sem("s_x0")
    s_x1 = P.sem("s_x1")
    s_x2 = P.sem("s_x2")
    s_x3 = P.sem("s_x3")
    s_m0 = P.sem("s_m0")
    s_m1 = P.sem("s_m1")
    s_m2 = P.sem("s_m2")
    s_m3 = P.sem("s_m3")
    s_m4 = P.sem("s_m4")
    s_th = P.sem("s_th")
    s_x4 = P.sem("s_x4")
    s_var = P.sem("s_var")
    s_rst = P.sem("s_rst")
    ds_x4 = P.dsem(2, 1, "dsx4")
    ds_qo = P.dsem(2, 1, "dsqo")
    ds_kv = P.dsem(2, 1, "dskv")

    def bcast(ap_in, k):
        return bass.AP(ap_in.tensor, 0, [[0, 128], [1, k]])

    with nc.Block() as blk:
        @blk.gpsimd
        def _(g):
            g = P.chain(g)
            g.dma_start(out=a2_sb[:, :],
                        in_=a2.rearrange("(c p) d -> p c d", p=128)
                        ).then_inc(s_wa2, 16)
            g.dma_start(out=wq_sb[:, :],
                        in_=WqT.rearrange("(c p) d -> p c d", p=128)
                        ).then_inc(s_wq, 16)

        @blk.sync
        def _(sy):
            sy = P.chain(sy)
            sy.dma_start(out=fbb[:, :], in_=bcast(fb, D)).then_inc(s_b, 16)
            sy.dma_start(out=bq8[:, :], in_=bcast(bq, D)).then_inc(s_b, 16)
            sy.dma_start(out=bkvb[:, :], in_=bcast(bkv, 128)).then_inc(s_b, 16)
            for tt in range(NT):
                sy.wait_ge(s_x4, tt + 1)
                sy.dma_start(out=x4_o[128 * tt:128 * (tt + 1), :],
                             in_=x4t[:, (tt % 2) * D:(tt % 2 + 1) * D]
                             ).then_inc(ds_x4.sem(tt), 16)
                sy.wait_ge(s_qe, tt + 1)
                sy.dma_start(out=q_o[128 * tt:128 * (tt + 1), :],
                             in_=qo[:, (tt % 2) * D:(tt % 2 + 1) * D]
                             ).then_inc(ds_qo.sem(tt), 16)
                sy.wait_ge(s_ke, tt + 1)
                sy.dma_start(out=kv_o[128 * tt:128 * (tt + 1), :],
                             in_=kvo[:, (tt % 2) * 128:(tt % 2 + 1) * 128]
                             ).then_inc(ds_kv.sem(tt), 16)
            ds_x4.wait_all(sy, NT)
            ds_qo.wait_all(sy, NT)
            ds_kv.wait_all(sy, NT)

        @blk.tensor
        def _(t):
            t = P.chain(t)
            t.wait_ge(s_w1, 48)

            def tp_round(slot):
                for j in range(8):
                    inst = t.transpose(ps_tp2[:, 128 * j:128 * (j + 1)],
                                       srcb[:, slot * D + 128 * j:
                                            slot * D + 128 * (j + 1)],
                                       P.ident[:, :])
                return inst

            def mm_round(slot, w_sb, psq, fis):
                wstride = w_sb.shape[1] // 8
                for f0, fw in fis:
                    for ci in range(8):
                        inst = t.matmul(
                            psq[:, f0:f0 + fw],
                            xT[:, slot * D + 128 * ci:
                               slot * D + 128 * ci + 128],
                            w_sb[:, ci * wstride + f0: ci * wstride + f0 + fw],
                            start=(ci == 0), stop=(ci == 7))
                return inst

            def tail(k):
                t.wait_ge(s_l2, k + 1)
                tp_round(3).then_inc(s_t3, 1)
                if k == 0:
                    t.wait_ge(s_wq, 16)
                t.wait_ge(s_x3, k + 1)
                if k == NT - 1:
                    t.wait_ge(s_qe, k)
                    t.wait_ge(s_ke, k)
                mm_round(3, wq_sb, ps_q[0], F_CH).then_inc(s_m3, 1)
                mm_round(3, wkv_sb, ps_q[1], [(0, 128)]).then_inc(s_m4, 1)

            for tt in range(NT):
                t.wait_ge(s_l1, tt + 1)
                if tt >= 2:
                    t.wait_ge(s_x3, tt - 1)
                elif tt == 1:
                    t.wait_ge(s_x2, 1)
                tp_round(0).then_inc(s_t0, 1)
                t.wait_ge(s_x0, tt + 1)
                if tt >= 2:
                    t.wait_ge(s_qe, tt - 1)
                mm_round(0, wf_sb, ps_q[0], F_CH).then_inc(s_m0, 1)
                t.wait_ge(s_h1, tt + 1)
                tp_round(1).then_inc(s_t1, 1)
                t.wait_ge(s_x1, tt + 1)
                if tt >= 2:
                    t.wait_ge(s_ke, tt - 1)
                mm_round(1, a1_sb, ps_q[1], F_CH).then_inc(s_m1, 1)
                t.wait_ge(s_h2, tt + 1)
                tp_round(2).then_inc(s_t2, 1)
                if tt == 0:
                    t.wait_ge(s_wa2, 16)
                t.wait_ge(s_x2, tt + 1)
                if tt >= 1:
                    t.wait_ge(s_h3, tt)
                mm_round(2, a2_sb, ps_q[2], F_CH).then_inc(s_m2, 1)
                if tt >= 1:
                    tail(tt - 1)
            tail(NT - 1)

        @blk.vector
        def _(v):
            v = P.chain(v)
            v.wait_ge(s_b, 48)
            v.tensor_scalar_mul(bq8[:, :], bq8[:, :], 0.125)

            def qkv_ev(k):
                v.wait_ge(s_m3, k + 1)
                if k >= 2:
                    ds_qo.wait(v, k - 2)
                v.scalar_tensor_tensor(qo[:, (k % 2) * D:(k % 2 + 1) * D],
                                       ps_q[0][:, :], 0.125, bq8[:, :],
                                       ALU.mult, ALU.add).then_inc(s_qe, 1)
                v.wait_ge(s_m4, k + 1)
                if k >= 2:
                    ds_kv.wait(v, k - 2)
                v.tensor_tensor(kvo[:, (k % 2) * 128:(k % 2 + 1) * 128],
                                ps_q[1][:, 0:128], bkvb[:, :],
                                ALU.add).then_inc(s_ke, 1)

            def ln1q(k):
                if k >= 1:
                    v.wait_ge(s_t0, k)
                v.tensor_scalar(srcb[:, 0:D], x3_sb[:, k * D:(k + 1) * D],
                                mean[:, k:k + 1], rstd[:, k:k + 1],
                                ALU.subtract, ALU.mult).then_inc(s_l1, 1)

            ln1q(0)
            if NT > 1:
                ln1q(1)
            for tt in range(NT):
                v.wait_ge(s_m2, tt + 1)
                if tt >= 1:
                    v.wait_ge(s_th, tt)
                v.tensor_tensor(h3f[:, :], ps_q[2][:, :],
                                fbb[:, :], ALU.add).then_inc(s_h3, 1)
                v.wait_ge(s_th, tt + 1)
                if tt >= 2:
                    ds_x4.wait(v, tt - 2)
                v.scalar_tensor_tensor(x4t[:, (tt % 2) * D:(tt % 2 + 1) * D],
                                       th[:, :], 0.5,
                                       x3_sb[:, tt * D:(tt + 1) * D],
                                       ALU.mult, ALU.add).then_inc(s_x4, 1)
                x4s = x4t[:, (tt % 2) * D:(tt % 2 + 1) * D]
                v.reduce_sum(st[:, 0:1], x4s, axis=AX)
                v.tensor_mul(h3f[:, :], x4s, x4s)
                v.reduce_sum(st[:, 1:2], h3f[:, :], axis=AX)
                v.tensor_scalar_mul(mt[:, 0:1], st[:, 0:1], 1.0 / D)
                if tt >= 1:
                    v.wait_ge(s_rst, tt)
                v.tensor_mul(st[:, 2:3], mt[:, 0:1], mt[:, 0:1])
                v.scalar_tensor_tensor(var_t[:, 0:1], st[:, 1:2], 1.0 / D,
                                       st[:, 2:3], ALU.mult,
                                       ALU.subtract).then_inc(s_var, 1)
                v.wait_ge(s_rst, tt + 1)
                if tt >= 1:
                    v.wait_ge(s_t3, tt)
                v.tensor_scalar(srcb[:, 3 * D:4 * D], x4s,
                                mt[:, 0:1], rt_t[:, 0:1],
                                ALU.subtract, ALU.mult).then_inc(s_l2, 1)
                if tt >= 1:
                    qkv_ev(tt - 1)
                if tt + 2 < NT:
                    ln1q(tt + 2)
            qkv_ev(NT - 1)

        @blk.scalar
        def _(s):
            s = P.chain(s)
            for tt in range(NT):
                s.wait_ge(s_t0, tt + 1)
                if tt >= 1:
                    s.wait_ge(s_m0, tt)
                s.copy(xT[:, 0:D], ps_tp2[:, :]).then_inc(s_x0, 1)
                s.wait_ge(s_m0, tt + 1)
                if tt >= 1:
                    s.wait_ge(s_t1, tt)
                s.activation(srcb[:, D:2 * D], ps_q[0][:, :],
                             AF.Identity).then_inc(s_h1, 1)
                s.wait_ge(s_t1, tt + 1)
                if tt >= 1:
                    s.wait_ge(s_m1, tt)
                s.copy(xT[:, D:2 * D], ps_tp2[:, :]).then_inc(s_x1, 1)
                s.wait_ge(s_m1, tt + 1)
                if tt >= 1:
                    s.wait_ge(s_t2, tt)
                s.activation(srcb[:, 2 * D:3 * D], ps_q[1][:, :],
                             AF.Identity).then_inc(s_h2, 1)
                s.wait_ge(s_t2, tt + 1)
                if tt >= 1:
                    s.wait_ge(s_m2, tt)
                s.copy(xT[:, 2 * D:3 * D], ps_tp2[:, :]).then_inc(s_x2, 1)
                s.wait_ge(s_h3, tt + 1)
                if tt >= 1:
                    s.wait_ge(s_x4, tt)
                s.activation(th[:, :], h3f[:, :], AF.Tanh).then_inc(s_th, 1)
                s.wait_ge(s_var, tt + 1)
                s.activation(lv_t[:, 0:1], var_t[:, 0:1], AF.Ln,
                             bias=eps_l[:, 0:1])
                if tt >= 1:
                    s.wait_ge(s_l2, tt)
                s.activation(rt_t[:, 0:1], lv_t[:, 0:1], AF.Exp,
                             scale=-0.5).then_inc(s_rst, 1)
                if tt >= 1:
                    s.wait_ge(s_t3, tt)
                    if tt >= 2:
                        s.wait_ge(s_m4, tt - 1)
                    s.copy(xT[:, 3 * D:4 * D], ps_tp2[:, :]).then_inc(s_x3, 1)
            s.wait_ge(s_t3, NT)
            s.copy(xT[:, 3 * D:4 * D], ps_tp2[:, :]).then_inc(s_x3, 1)
    P.pop_region()
    return P.close()



SK = 8192
NKT = SK // 128


def build_L9():
    P = Prog()
    nc = P.nc
    q_in = P.din("qbf", [S, D], BF16)
    K_in = P.din("Kb", [SK, 64], BF16)
    V_in = P.din("Vb", [SK, 64], BF16)
    slab_o = P.dout("slab", [4 * 16 * 512, 64], BF16)

    emit_ident(P)

    qT0 = P.sb([128, 8 * S], BF16, "qT0")
    krep = P.sb([128, NKT * 128], BF16, "krep")
    vhat = P.sb([128, NKT * 65], BF16, "vhat")
    nmb = P.sb([128, 1], F32, "nmb")
    identf = P.sb([128, 128], F32, "identf")

    P.push_region()
    qtile = P.sb([128, 2 * D], BF16)
    ktile = P.sb([128, 2 * 512], BF16)
    sqf = P.sb([128, D], F32)
    n16 = P.sb([128, 18], F32)
    qnacc = P.sb([128, 1], F32)
    knacc = P.sb([128, 1], F32)
    mxs = P.sb([1, 8], F32)
    ps_tq = P.ps([128, 1024], BF16, "ps_tq")
    ps_tk = P.ps([64, 1024], BF16, "ps_tk")

    ds_q = P.dsem(2, 1, "dsq9")
    ds_k = P.dsem(2, 1, "dsk9")
    s_vh = P.sem("s_vh")
    s_z = P.sem("s_z")
    s_ms = P.sem("s_ms")
    s_tq = P.sem("s_tq")
    s_eq = P.sem("s_eq")
    s_tk = P.sem("s_tk")
    s_ek = P.sem("s_ek")
    s_nq = P.sem("s_nq")
    s_nk = P.sem("s_nk")

    with nc.Block() as blk:
        @blk.sync
        def _(sy):
            sy = P.chain(sy)
            for tt in range(NT):
                if tt >= 2:
                    sy.wait_ge(s_nq, tt - 1)
                    sy.wait_ge(s_tq, tt - 1)
                sy.dma_start(out=qtile[:, (tt % 2) * D:(tt % 2 + 1) * D],
                             in_=q_in[128 * tt:128 * (tt + 1), :]
                             ).then_inc(ds_q.sem(tt), 16)
            for r in range(8):
                if r >= 2:
                    sy.wait_ge(s_ek, r - 1)
                    sy.wait_ge(s_nk, r - 1)
                sy.dma_start(
                    out=ktile[:, (r % 2) * 512:(r % 2 + 1) * 512],
                    in_=bass.AP(K_in.tensor, 1024 * r * 64,
                                [[64, 128], [128 * 64, 8], [1, 64]])
                ).then_inc(ds_k.sem(r), 16)
            sy.wait_ge(s_ms, 1)
            for i in range(NKT):
                sy.dma_start(out=vhat[:, 65 * i:65 * i + 64],
                             in_=V_in[128 * i:128 * (i + 1), :]
                             ).then_inc(s_vh, 16)

        @blk.gpsimd
        def _(g):
            g = P.chain(g)
            g.memset(vhat[:, :], 1.0).then_inc(s_ms, 1)
            g.memset(qnacc[:, :], 0.0)
            g.memset(knacc[:, :], 0.0).then_inc(s_z, 1)

        @blk.tensor
        def _(t):
            t = P.chain(t)
            for tt in range(NT):
                ds_q.wait(t, tt)
                if tt >= 1:
                    t.wait_ge(s_eq, tt)
                for j in range(8):
                    inst = t.transpose(ps_tq[:, 128 * j:128 * (j + 1)],
                                       qtile[:, (tt % 2) * D + 128 * j:
                                             (tt % 2) * D + 128 * (j + 1)],
                                       P.ident[:, :])
                inst.then_inc(s_tq, 1)
            for r in range(8):
                ds_k.wait(t, r)
                if r >= 1:
                    t.wait_ge(s_ek, r)
                for k in range(8):
                    inst = t.transpose(ps_tk[:, 128 * k:128 * (k + 1)],
                                       ktile[0:128, (r % 2) * 512 + 64 * k:
                                             (r % 2) * 512 + 64 * (k + 1)],
                                       P.ident[:, :])
                inst.then_inc(s_tk, 1)

        @blk.scalar
        def _(s):
            s = P.chain(s)
            for tt in range(NT):
                s.wait_ge(s_tq, tt + 1)
                s.copy(sb_ap3(qT0, 128 * tt, S, 8, 128),
                       sb_ap3(ps_tq, 0, 128, 8, 128)).then_inc(s_eq, 1)
            for r in range(8):
                s.wait_ge(s_tk, r + 1)
                s.copy(krep[0:64, 1024 * r:1024 * (r + 1)], ps_tk[0:64, :])
                s.copy(krep[64:128, 1024 * r:1024 * (r + 1)],
                       ps_tk[0:64, :]).then_inc(s_ek, 1)

        @blk.vector
        def _(v):
            v = P.chain(v)
            v.wait_ge(s_z, 1)
            for tt in range(NT):
                ds_q.wait(v, tt)
                qs = qtile[:, (tt % 2) * D:(tt % 2 + 1) * D]
                v.tensor_mul(sqf[:, :], qs, qs)
                for vv in range(16):
                    v.reduce_sum(n16[:, vv:vv + 1],
                                 sqf[:, 64 * vv:64 * (vv + 1)], axis=AX)
                v.reduce_max(n16[:, 16:17], n16[:, 0:16], axis=AX)
                v.tensor_max(qnacc[:, 0:1], qnacc[:, 0:1],
                             n16[:, 16:17]).then_inc(s_nq, 1)
            for r in range(8):
                ds_k.wait(v, r)
                ks = ktile[:, (r % 2) * 512:(r % 2 + 1) * 512]
                v.tensor_mul(sqf[:, 0:512], ks, ks)
                for i in range(8):
                    v.reduce_sum(n16[:, i:i + 1],
                                 sqf[:, 64 * i:64 * (i + 1)], axis=AX)
                v.reduce_max(n16[:, 16:17], n16[:, 0:8], axis=AX)
                v.tensor_max(knacc[:, 0:1], knacc[:, 0:1],
                             n16[:, 16:17]).then_inc(s_nk, 1)

    emit_partition_max(P, qnacc, mxs[0:1, 0:1])
    emit_partition_max(P, knacc, mxs[0:1, 1:2])
    with nc.Block() as blk:
        @blk.scalar
        def _(s):
            s = P.chain(s)
            s.activation(mxs[0:1, 2:3], mxs[0:1, 0:1], AF.Ln)
            s.activation(mxs[0:1, 3:4], mxs[0:1, 1:2], AF.Ln)
    with nc.Block() as blk:
        @blk.vector
        def _(v):
            v = P.chain(v)
            v.tensor_add(mxs[0:1, 4:5], mxs[0:1, 2:3], mxs[0:1, 3:4])
    with nc.Block() as blk:
        @blk.scalar
        def _(s):
            s = P.chain(s)
            s.activation(mxs[0:1, 5:6], mxs[0:1, 4:5], AF.Exp, scale=0.5)
    with nc.Block() as blk:
        @blk.vector
        def _(v):
            v = P.chain(v)
            v.tensor_scalar_mul(mxs[0:1, 6:7], mxs[0:1, 5:6], -1.0)
            v.tensor_copy(identf[:, :], P.ident[:, :])
    emit_bounce_bcast(P, mxs[0:1, 6:7], 1, nmb)
    with nc.Block() as blk:
        @blk.sync
        def _(sy):
            sy = P.chain(sy)
            sy.wait_ge(s_vh, 16 * NKT)
    P.pop_region()

    P.push_region()
    et = P.sb([128, 2 * 1024], BF16, "et")
    pvt = P.sb([65, 2 * 1024], F32, "pvt")
    ob = P.sb([128, 2 * 512], BF16, "ob")
    rc = P.sb([128, 1], F32)
    ps_s = [P.ps([128, 1024], F32, f"ps_s{i}") for i in range(2)]
    ps_pv = P.ps([65, 1024], F32, "ps_pv")
    ps_ep = P.ps([128, 1024], F32, "ps_ep")

    s_qk = P.sem("s_qk")
    s_ex = P.sem("s_ex")
    s_pv = P.sem("s_pv")
    s_pvev = P.sem("s_pvev")
    s_eptp = P.sem("s_eptp")
    s_epv = P.sem("s_epv")
    ds_sl = P.dsem(2, 8, "dssl")

    NIT = 32

    def emit_pv(t, ks):
        it, sc = divmod(ks, 64)
        if sc == 0 and it >= 1:
            t.wait_ge(s_pvev, it)
        t.wait_ge(s_ex, ks + 1)
        for hf in (0, 512):
            inst = t.matmul(ps_pv[:, hf:hf + 512],
                            vhat[:, 65 * sc:65 * (sc + 1)],
                            et[:, (ks % 2) * 1024 + hf:
                               (ks % 2) * 1024 + hf + 512],
                            start=(sc == 0), stop=(sc == 63))
        inst.then_inc(s_pv, 1)

    def emit_eptp(t, it):
        t.wait_ge(s_pvev, it + 1)
        if it >= 1:
            t.wait_ge(s_epv, it)
        for jj in range(8):
            inst = t.transpose(ps_ep[:, 128 * jj:128 * jj + 65],
                               pvt[0:65, (it % 2) * 1024 + 128 * jj:
                                   (it % 2) * 1024 + 128 * (jj + 1)],
                               identf[0:65, 0:65])
        inst.then_inc(s_eptp, 1)

    with nc.Block() as blk:
        @blk.tensor
        def _(t):
            t = P.chain(t)
            for it in range(NIT):
                hpp, j = divmod(it, 8)
                qcol = j * S + 512 * hpp
                for sc in range(64):
                    ks = it * 64 + sc
                    if ks >= 2:
                        t.wait_ge(s_ex, ks - 1)
                    t.matmul(ps_s[ks % 2][:, 0:512],
                             krep[0:64, 128 * sc:128 * (sc + 1)],
                             qT0[0:64, qcol:qcol + 512],
                             start=True, stop=True, tile_position=(0, 0))
                    t.matmul(ps_s[ks % 2][:, 512:1024],
                             krep[64:128, 128 * sc:128 * (sc + 1)],
                             qT0[64:128, qcol:qcol + 512],
                             start=True, stop=True,
                             tile_position=(64, 0)).then_inc(s_qk, 1)
                    if ks >= 1:
                        emit_pv(t, ks - 1)
                        if ks % 64 == 0:
                            emit_eptp(t, it - 1)
            emit_pv(t, NIT * 64 - 1)
            emit_eptp(t, NIT - 1)

        @blk.scalar
        def _(s):
            s = P.chain(s)
            for it in range(NIT):
                for sc in range(64):
                    ks = it * 64 + sc
                    s.wait_ge(s_qk, ks + 1)
                    if ks >= 2:
                        s.wait_ge(s_pv, ks - 1)
                    s.activation(et[:, (ks % 2) * 1024:(ks % 2 + 1) * 1024],
                                 ps_s[ks % 2][:, :], AF.Exp,
                                 bias=nmb[:, 0:1]).then_inc(s_ex, 1)
                s.wait_ge(s_pv, 64 * (it + 1))
                if it >= 2:
                    s.wait_ge(s_eptp, it - 1)
                s.copy(pvt[0:65, (it % 2) * 1024:(it % 2 + 1) * 1024],
                       ps_pv[0:65, :]).then_inc(s_pvev, 1)

        @blk.vector
        def _(v):
            v = P.chain(v)
            for it in range(NIT):
                v.wait_ge(s_eptp, it + 1)
                if it >= 2:
                    ds_sl.wait(v, it - 2)
                for jj in range(8):
                    v.reciprocal(rc[:, 0:1], ps_ep[:, 128 * jj + 64:128 * jj + 65])
                    inst = v.tensor_scalar(
                        ob[:, (it % 2) * 512 + 64 * jj:
                           (it % 2) * 512 + 64 * (jj + 1)],
                        ps_ep[:, 128 * jj:128 * jj + 64],
                        rc[:, 0:1], None, ALU.mult)
                inst.then_inc(s_epv, 1)

        @blk.sync
        def _(sy):
            sy = P.chain(sy)
            for it in range(NIT):
                hpp, j = divmod(it, 8)
                sy.wait_ge(s_epv, it + 1)
                for jj in range(8):
                    vv = 2 * j + (jj // 4)
                    cblk = jj % 4
                    rb = (hpp * 16 + vv) * 512 + 128 * cblk
                    sy.dma_start(out=slab_o[rb:rb + 128, :],
                                 in_=ob[:, (it % 2) * 512 + 64 * jj:
                                        (it % 2) * 512 + 64 * (jj + 1)]
                                 ).then_inc(ds_sl.sem(it), 16)
            ds_sl.wait_all(sy, NIT)
    P.pop_region()
    return P.close()



F_CH = chunks_of(D, 512)


def build_L10():
    P = Prog()
    nc = P.nc
    x4_in = P.din("x4", [S, D])
    at_in = P.din("attn", [S, D], BF16)
    WoT = P.din("WoT", [D, D])
    bo = P.din("bo", [1, D])
    wf = P.din("wf", [D, D])
    a1 = P.din("a1", [D, D])
    a2 = P.din("a2", [D, D])
    fb = P.din("fb", [1, D])
    xo_o = P.dout("xo", [S, D])

    emit_ident(P)
    eps_l = P.constf(1e-5)

    wo_sb = P.sb([128, 8 * D], BF16, "wo_sb")
    wf_sb = P.sb([128, 8 * D], BF16, "wf_sb")
    a1_sb = P.sb([128, 8 * D], BF16, "a1_sb")
    a2_sb = P.sb([128, 8 * D], BF16, "a2_sb")
    bob = P.sb([128, D], F32, "bob")
    fbb = P.sb([128, D], F32, "fbb")
    atile = P.sb([128, 2 * D], BF16, "atile")
    x4i = P.sb([128, 2 * D], F32, "x4i")
    x5t = P.sb([128, 2 * D], F32, "x5t")
    xot = P.sb([128, 2 * D], F32, "xot")
    srcb = P.sb([128, 2 * D], BF16, "srcb")
    xT = P.sb([128, 2 * D], BF16, "xT10")
    h3f = P.sb([128, D], F32, "h3f")
    th = P.sb([128, D], F32, "th10")
    st = P.sb([128, 4], F32)
    mt = P.sb([128, 1], F32)
    var_t = P.sb([128, 1], F32)
    lv_t = P.sb([128, 1], F32)
    rt_t = P.sb([128, 1], F32)

    ps_tp2 = P.ps([128, 1024], BF16, "ps_tp2")
    ps_q = [P.ps([128, 1024], F32, f"psq{i}") for i in range(3)]

    s_w = P.sem("s_w")
    s_b = P.sem("s_b")
    s_src = P.sem("s_src")
    s_tp = P.sem("s_tp")
    s_xe = P.sem("s_xe")
    s_mmr = P.sem("s_mmr")
    s_pevA = P.sem("s_pevA")
    s_pevV = P.sem("s_pevV")
    s_th = P.sem("s_th")
    s_xo = P.sem("s_xo")
    s_var = P.sem("s_var")
    s_rst = P.sem("s_rst")
    ds_at = P.dsem(2, 1, "dsat")
    ds_xi = P.dsem(2, 1, "dsxi")
    ds_o = P.dsem(2, 1, "dso10")

    def bcast(ap_in, k):
        return bass.AP(ap_in.tensor, 0, [[0, 128], [1, k]])

    with nc.Block() as blk:
        @blk.gpsimd
        def _(g):
            g = P.chain(g)
            for w_in, w_sb in ((WoT, wo_sb), (wf, wf_sb), (a1, a1_sb),
                               (a2, a2_sb)):
                g.dma_start(out=w_sb[:, :],
                            in_=w_in.rearrange("(c p) d -> p c d", p=128)
                            ).then_inc(s_w, 16)

        @blk.sync
        def _(sy):
            sy = P.chain(sy)
            sy.dma_start(out=bob[:, :], in_=bcast(bo, D)).then_inc(s_b, 16)
            sy.dma_start(out=fbb[:, :], in_=bcast(fb, D)).then_inc(s_b, 16)
            for tt in range(NT + 2):
                if tt < NT:
                    if tt >= 2:
                        sy.wait_ge(s_tp, 4 * tt - 7)
                    sy.dma_start(out=atile[:, (tt % 2) * D:(tt % 2 + 1) * D],
                                 in_=at_in[128 * tt:128 * (tt + 1), :]
                                 ).then_inc(ds_at.sem(tt), 16)
                    if tt >= 2:
                        sy.wait_ge(s_pevV, 2 * tt - 3)
                    sy.dma_start(out=x4i[:, (tt % 2) * D:(tt % 2 + 1) * D],
                                 in_=x4_in[128 * tt:128 * (tt + 1), :]
                                 ).then_inc(ds_xi.sem(tt), 16)
                if tt >= 2:
                    to = tt - 2
                    sy.wait_ge(s_xo, to + 1)
                    sy.dma_start(out=xo_o[128 * to:128 * (to + 1), :],
                                 in_=xot[:, (to % 2) * D:(to % 2 + 1) * D]
                                 ).then_inc(ds_o.sem(to), 16)
            ds_o.wait_all(sy, NT)

        @blk.tensor
        def _(t):
            t = P.chain(t)

            def tp_round(src, c0):
                for j in range(8):
                    inst = t.transpose(ps_tp2[:, 128 * j:128 * (j + 1)],
                                       src[:, c0 + 128 * j:c0 + 128 * (j + 1)],
                                       P.ident[:, :])
                return inst

            def mm_round(slot, w_sb, psq):
                for f0, fw in F_CH:
                    for ci in range(8):
                        inst = t.matmul(
                            psq[:, f0:f0 + fw],
                            xT[:, slot * D + 128 * ci:
                               slot * D + 128 * ci + 128],
                            w_sb[:, ci * D + f0: ci * D + f0 + fw],
                            start=(ci == 0), stop=(ci == 7))
                return inst

            for tt in range(NT):
                k4 = 4 * tt
                ds_at.wait(t, tt)
                if k4 >= 1:
                    t.wait_ge(s_xe, k4)
                tp_round(atile, (tt % 2) * D).then_inc(s_tp, 1)
                if tt == 0:
                    t.wait_ge(s_w, 64)
                t.wait_ge(s_xe, k4 + 1)
                if tt >= 1:
                    t.wait_ge(s_pevA, 2 * tt - 1)
                mm_round(k4 % 2, wo_sb, ps_q[k4 % 3]).then_inc(s_mmr, 1)
                t.wait_ge(s_src, tt + 1)
                tp_round(srcb, ((k4 + 1) % 2) * D).then_inc(s_tp, 1)
                t.wait_ge(s_xe, k4 + 2)
                if tt >= 1:
                    t.wait_ge(s_pevA, 2 * tt)
                mm_round((k4 + 1) % 2, wf_sb, ps_q[(k4 + 1) % 3]
                         ).then_inc(s_mmr, 1)
                t.wait_ge(s_pevA, 2 * tt + 1)
                tp_round(srcb, ((k4 + 2) % 2) * D).then_inc(s_tp, 1)
                t.wait_ge(s_xe, k4 + 3)
                if tt >= 1:
                    t.wait_ge(s_pevV, 2 * tt)
                mm_round((k4 + 2) % 2, a1_sb, ps_q[(k4 + 2) % 3]
                         ).then_inc(s_mmr, 1)
                t.wait_ge(s_pevA, 2 * tt + 2)
                tp_round(srcb, ((k4 + 3) % 2) * D).then_inc(s_tp, 1)
                t.wait_ge(s_xe, k4 + 4)
                t.wait_ge(s_pevV, 2 * tt + 1)
                mm_round((k4 + 3) % 2, a2_sb, ps_q[(k4 + 3) % 3]
                         ).then_inc(s_mmr, 1)

        @blk.vector
        def _(v):
            v = P.chain(v)
            v.wait_ge(s_b, 32)
            for tt in range(NT):
                k4 = 4 * tt
                x5s = x5t[:, (tt % 2) * D:(tt % 2 + 1) * D]
                v.wait_ge(s_mmr, k4 + 1)
                ds_xi.wait(v, tt)
                v.tensor_tensor(x5s, ps_q[k4 % 3][:, :], bob[:, :], ALU.add)
                v.tensor_tensor(x5s, x5s, x4i[:, (tt % 2) * D:(tt % 2 + 1) * D],
                                ALU.add).then_inc(s_pevV, 1)
                if tt >= 1:
                    v.wait_ge(s_th, tt)
                v.reduce_sum(st[:, 0:1], x5s, axis=AX)
                v.tensor_mul(h3f[:, :], x5s, x5s)
                v.reduce_sum(st[:, 1:2], h3f[:, :], axis=AX)
                v.tensor_scalar_mul(mt[:, 0:1], st[:, 0:1], 1.0 / D)
                if tt >= 1:
                    v.wait_ge(s_rst, tt)
                v.tensor_mul(st[:, 2:3], mt[:, 0:1], mt[:, 0:1])
                v.scalar_tensor_tensor(var_t[:, 0:1], st[:, 1:2], 1.0 / D,
                                       st[:, 2:3], ALU.mult,
                                       ALU.subtract).then_inc(s_var, 1)
                v.wait_ge(s_rst, tt + 1)
                if k4 >= 1:
                    v.wait_ge(s_tp, k4)
                v.tensor_scalar(srcb[:, ((k4 + 1) % 2) * D:
                                     ((k4 + 1) % 2 + 1) * D],
                                x5s, mt[:, 0:1], rt_t[:, 0:1],
                                ALU.subtract, ALU.mult).then_inc(s_src, 1)
                v.wait_ge(s_mmr, k4 + 4)
                v.tensor_tensor(h3f[:, :], ps_q[(k4 + 3) % 3][:, :],
                                fbb[:, :], ALU.add).then_inc(s_pevV, 1)
                v.wait_ge(s_th, tt + 1)
                if tt >= 2:
                    ds_o.wait(v, tt - 2)
                v.scalar_tensor_tensor(xot[:, (tt % 2) * D:(tt % 2 + 1) * D],
                                       th[:, :], 0.5, x5s,
                                       ALU.mult, ALU.add).then_inc(s_xo, 1)

        @blk.scalar
        def _(s):
            s = P.chain(s)
            for tt in range(NT):
                k4 = 4 * tt
                s.wait_ge(s_tp, k4 + 1)
                if k4 >= 2:
                    s.wait_ge(s_mmr, k4 - 1)
                s.copy(xT[:, (k4 % 2) * D:(k4 % 2 + 1) * D],
                       ps_tp2[:, :]).then_inc(s_xe, 1)
                s.wait_ge(s_var, tt + 1)
                s.activation(lv_t[:, 0:1], var_t[:, 0:1], AF.Ln,
                             bias=eps_l[:, 0:1])
                if tt >= 1:
                    s.wait_ge(s_src, tt)
                s.activation(rt_t[:, 0:1], lv_t[:, 0:1], AF.Exp,
                             scale=-0.5).then_inc(s_rst, 1)
                s.wait_ge(s_tp, k4 + 2)
                if k4 >= 1:
                    s.wait_ge(s_mmr, k4)
                s.copy(xT[:, ((k4 + 1) % 2) * D:((k4 + 1) % 2 + 1) * D],
                       ps_tp2[:, :]).then_inc(s_xe, 1)
                s.wait_ge(s_mmr, k4 + 2)
                if k4 >= 1:
                    s.wait_ge(s_tp, k4 - 1)
                s.activation(srcb[:, ((k4 + 2) % 2) * D:
                                  ((k4 + 2) % 2 + 1) * D],
                             ps_q[(k4 + 1) % 3][:, :],
                             AF.Identity).then_inc(s_pevA, 1)
                s.wait_ge(s_tp, k4 + 3)
                s.wait_ge(s_mmr, k4 + 1)
                s.copy(xT[:, (k4 % 2) * D:(k4 % 2 + 1) * D],
                       ps_tp2[:, :]).then_inc(s_xe, 1)
                s.wait_ge(s_mmr, k4 + 3)
                s.wait_ge(s_tp, k4 + 2)
                s.activation(srcb[:, ((k4 + 3) % 2) * D:
                                  ((k4 + 3) % 2 + 1) * D],
                             ps_q[(k4 + 2) % 3][:, :],
                             AF.Identity).then_inc(s_pevA, 1)
                s.wait_ge(s_tp, k4 + 4)
                s.wait_ge(s_mmr, k4 + 2)
                s.copy(xT[:, ((k4 + 3) % 2) * D:((k4 + 3) % 2 + 1) * D],
                       ps_tp2[:, :]).then_inc(s_xe, 1)
                s.wait_ge(s_pevV, 2 * tt + 2)
                if tt >= 1:
                    s.wait_ge(s_xo, tt)
                s.activation(th[:, :], h3f[:, :], AF.Tanh).then_inc(s_th, 1)
    return P.close()



_PROGS = {}


def _prog(name, builder):
    if name not in _PROGS:
        _PROGS[name] = builder()
    return _PROGS[name]


def _cc(a):
    return np.ascontiguousarray(a)


def kernel(**ins):
    f = np.float32
    bf = ml_dtypes.bfloat16
    ins = {k: np.asarray(v) for k, v in ins.items()}
    cores = list(range(8))
    x = _cc(ins["L0_attn_x"].astype(f, copy=False).reshape(16384, 1024))
    xsh = [_cc(x[2048 * c:2048 * (c + 1)]) for c in cores]

    clip_x = _cc(ins["L0_ffn_clip_x"].astype(f, copy=False).reshape(1, 1))
    clip_d = _cc(ins["L0_ffn_clip_down"].astype(f, copy=False).reshape(1, 1))
    wg = _cc(ins["L0_ffn_w_gate"].astype(f, copy=False))
    wu = _cc(ins["L0_ffn_w_up"].astype(f, copy=False))
    wdn = _cc(ins["L0_ffn_w_down"].astype(f, copy=False))

    res = run_bass_kernel_spmd(_prog("L1", build_L1),
                               [{"xs": xsh[c]} for c in cores], core_ids=cores)
    M1 = _cc(np.concatenate([res.results[c]["m1"].reshape(1, 1)
                             for c in cores], 1).astype(f))

    im = [{"xs": xsh[c], "m8": M1, "clip_x": clip_x, "wg": wg, "wu": wu}
          for c in cores]
    res = run_bass_kernel_spmd(_prog("L2", build_L2), im, core_ids=cores)
    hsh = [_cc(res.results[c]["h"]) for c in cores]
    M2 = _cc(np.concatenate([res.results[c]["m2"].reshape(1, 1)
                             for c in cores], 1).astype(f))

    im = [{"xs": xsh[c], "h": hsh[c], "m8": M2, "clip_d": clip_d, "wd": wdn}
          for c in cores]
    res = run_bass_kernel_spmd(_prog("L3", build_L3), im, core_ids=cores)
    x1sh = [_cc(res.results[c]["x1"]) for c in cores]

    aux = np.zeros((128, 112), f)
    cw = ins["L0_attn_conv_w"].astype(f, copy=False).reshape(2048, 4)
    cb = ins["L0_attn_conv_b"].astype(f, copy=False).reshape(2048)
    Dp = ins["L0_attn_Dp"].astype(f, copy=False).reshape(2048)
    bdt = ins["L0_attn_b_dt"].astype(f, copy=False).reshape(2048)
    for j in range(16):
        for k in range(4):
            aux[:, 4 * j + k] = cw[128 * j:128 * (j + 1), k]
        aux[:, 64 + j] = cb[128 * j:128 * (j + 1)]
        aux[:, 80 + j] = Dp[128 * j:128 * (j + 1)]
        aux[:, 96 + j] = bdt[128 * j:128 * (j + 1)]
    w_inT = _cc(ins["L0_attn_w_in"].astype(f, copy=False).T)
    w_xprojT = _cc(ins["L0_attn_w_xproj"].astype(f, copy=False).T)
    w_dtT = _cc(ins["L0_attn_w_dt"].astype(f, copy=False).T)
    w_outT = _cc(ins["L0_attn_w_out"].astype(f, copy=False).T)
    im = []
    for c in cores:
        halo = (np.zeros((128, 1024), f) if c % 4 == 0
                else x1sh[c - 1][-128:])
        im.append({"x1h": _cc(np.concatenate([halo, x1sh[c]], 0)),
                   "w_inT": w_inT, "w_xprojT": w_xprojT, "w_dtT": w_dtT,
                   "w_outT": w_outT, "aux": aux})
    res = run_bass_kernel_spmd(_prog("L4", build_L4), im, core_ids=cores)
    r4 = [res.results[c] for c in cores]

    dT_all = np.concatenate([r4[c]["deltaT"] for c in cores], 1)
    duT_all = np.concatenate([r4[c]["duT"] for c in cores], 1)
    B_all = np.zeros((2, 16, 8192), f)
    C_all = np.zeros((2, 16, 8192), f)
    for c in cores:
        b, tloc = divmod(2048 * c, 8192)
        B_all[b, :, tloc:tloc + 2048] = r4[c]["BC"][:16]
        C_all[b, :, tloc:tloc + 2048] = r4[c]["BC"][16:]
    E = np.zeros((128, 2048), f)
    ET = np.zeros((128, 2048), f)
    for db in range(16):
        for lane in range(128):
            r = 8 * db + lane // 16
            E[r, db * 128 + lane] = 1.0
            ET[lane, db * 128 + r] = 1.0
    A_log = ins["L0_attn_A_log"].astype(f, copy=False)
    Bb = np.zeros((2, 128, 8192), f)
    Cb = np.zeros((2, 128, 8192), f)
    for lane in range(128):
        Bb[:, lane, :] = B_all[:, lane % 16, :]
        Cb[:, lane, :] = C_all[:, lane % 16, :]
    im = []
    for c in cores:
        A_lane = np.zeros((128, 32), f)
        for g in range(2):
            for db in range(16):
                for lane in range(128):
                    A_lane[lane, g * 16 + db] = A_log[
                        256 * c + 128 * g + 8 * db + lane // 16, lane % 16]
        im.append({"dS": _cc(dT_all[256 * c:256 * (c + 1)]),
                   "duS": _cc(duT_all[256 * c:256 * (c + 1)]),
                   "Bb": Bb, "Cb": Cb, "A_lane": A_lane, "E": E, "ET": ET})
    res = run_bass_kernel_spmd(_prog("L5", build_L5), im, core_ids=cores)
    ys_all = np.concatenate([res.results[c]["ys"] for c in cores], 0)

    n1w = _cc(ins["L0_attn_norm1_w"].astype(f, copy=False).reshape(1, -1))
    n2w = _cc(ins["L0_attn_norm2_w"].astype(f, copy=False).reshape(1, -1))
    w1 = _cc(ins["L0_attn_ffn_w1"].astype(f, copy=False))
    b1 = _cc(ins["L0_attn_ffn_b1"].astype(f, copy=False).reshape(1, -1))
    w2 = _cc(ins["L0_attn_ffn_w2"].astype(f, copy=False))
    b2 = _cc(ins["L0_attn_ffn_b2"].astype(f, copy=False).reshape(1, -1))
    im = []
    for c in cores:
        im.append({"x1": x1sh[c],
                   "ysT": _cc(ys_all[:, 2048 * c:2048 * (c + 1)]),
                   "szT": _cc(r4[c]["szT"]), "part2": _cc(r4[c]["part2"]),
                   "w_outT": w_outT, "n1w": n1w, "n2w": n2w,
                   "w1": w1, "b1": b1, "w2": w2, "b2": b2})
    res = run_bass_kernel_spmd(_prog("L6", build_L6), im, core_ids=cores)
    x2sh = [_cc(res.results[c]["x2"]) for c in cores]
    M3 = _cc(np.concatenate([res.results[c]["m3"].reshape(1, 1)
                             for c in cores], 1).astype(f))

    im = [{"xs": x2sh[c], "m8": M3, "clip_x": clip_x, "wg": wg, "wu": wu}
          for c in cores]
    res = run_bass_kernel_spmd(_prog("L2", build_L2), im, core_ids=cores)
    h7sh = [_cc(res.results[c]["h"]) for c in cores]
    M4 = _cc(np.concatenate([res.results[c]["m2"].reshape(1, 1)
                             for c in cores], 1).astype(f))

    wf = _cc(ins["L1_ffn_weight"].astype(f, copy=False))
    a1w = _cc(ins["L1_ffn_aux_weight1"].astype(f, copy=False))
    a2w = _cc(ins["L1_ffn_aux_weight2"].astype(f, copy=False))
    fbv = _cc(ins["L1_ffn_bias"].astype(f, copy=False).reshape(1, -1))
    WqT = _cc(ins["L1_attn_Wq"].astype(f, copy=False).T)
    bqv = _cc(ins["L1_attn_bq"].astype(f, copy=False).reshape(1, -1))
    WkvT = _cc(ins["L1_attn_Wkv"].astype(f, copy=False).T)
    bkvv = _cc(ins["L1_attn_bkv"].astype(f, copy=False).reshape(1, -1))
    im = [{"xs": x2sh[c], "h": h7sh[c], "m8": M4, "clip_d": clip_d,
           "wd": wdn, "wf": wf, "a1": a1w, "a2": a2w, "fb": fbv,
           "WqT": WqT, "bq": bqv, "WkvT": WkvT, "bkv": bkvv}
          for c in cores]
    res = run_bass_kernel_spmd(_prog("L8", build_L8), im, core_ids=cores)
    x4sh = [_cc(res.results[c]["x4"]) for c in cores]
    qsh = [_cc(res.results[c]["qbf"]) for c in cores]
    kvsh = [_cc(res.results[c]["kvbf"]) for c in cores]

    im = []
    for c in cores:
        b = c // 4
        Kb = _cc(np.concatenate([kvsh[4 * b + i][:, 0:64] for i in range(4)], 0))
        Vb = _cc(np.concatenate([kvsh[4 * b + i][:, 64:128] for i in range(4)], 0))
        im.append({"qbf": qsh[c], "Kb": Kb, "Vb": Vb})
    res = run_bass_kernel_spmd(_prog("L9", build_L9), im, core_ids=cores)
    slabs = [res.results[c]["slab"] for c in cores]

    attn_full = np.zeros((2, 8192, 1024), bf)
    for c in cores:
        b, c0 = divmod(c, 4)
        s4 = slabs[c].reshape(4, 16, 512, 64).transpose(0, 2, 1, 3
                                                        ).reshape(4, 8192, 64)
        for hpp in range(4):
            attn_full[b, :, 64 * (4 * c0 + hpp):64 * (4 * c0 + hpp + 1)] = s4[hpp]
    WoT = _cc(ins["L1_attn_Wo"].astype(f, copy=False).T)
    bov = _cc(ins["L1_attn_bo"].astype(f, copy=False).reshape(1, -1))
    im = []
    for c in cores:
        b, c0 = divmod(c, 4)
        im.append({"x4": x4sh[c],
                   "attn": _cc(attn_full[b, 2048 * c0:2048 * (c0 + 1)]),
                   "WoT": WoT, "bo": bov, "wf": wf, "a1": a1w, "a2": a2w,
                   "fb": fbv})
    res = run_bass_kernel_spmd(_prog("L10", build_L10), im, core_ids=cores)
    xo = np.concatenate([res.results[c]["xo"] for c in cores], 0)
    return xo.reshape(2, 8192, 1024).astype(np.float32, copy=False)
